# version 24
# baseline (speedup 1.0000x reference)
import numpy as np
from contextlib import ExitStack

import concourse.bass as bass
import concourse.mybir as mybir
from concourse.alu_op_type import AluOpType as OP
from concourse.bass_core_rust import IndirectOffsetOnAxis
from concourse.bass_utils import run_bass_kernel_spmd

AF = mybir.ActivationFunctionType
AX = mybir.AxisListType.X
F32 = mybir.dt.float32
BF16 = mybir.dt.bfloat16
I32 = mybir.dt.int32
U32 = mybir.dt.uint32
U8 = mybir.dt.uint8
FP8 = mybir.dt.float8e4

NEG = -1.0e30
BIGI = 65536.0


class CFG:
    def __init__(self, D=1024, N=16384):
        self.D = D
        self.SD = D // 128
        self.N = N
        self.T = N // 128
        self.TO = self.T // 4
        self.NP = N // 512
        self.NG = self.TO // 4
        assert self.TO % 4 == 0 and self.SD >= 2 and self.NP >= 2 and self.NG >= 1


class Chain:

    def __init__(self, nc, ctx, name):
        self.s = ctx.enter_context(nc.semaphore(name))
        self.n = 0

    def __call__(self, instr, w=True):
        if w and self.n > 0:
            instr._wait_ge(self.s, self.n)
        instr.then_inc(self.s, 1)
        self.n += 1
        return self.n

    def signal(self, eng, sem, k=1):
        i = eng.sem_inc(sem, k)
        if self.n > 0:
            i._wait_ge(self.s, self.n)
        return i


def build_program(c: CFG):
    nc = bass.Bass("TRN2", target_bir_lowering=False, debug=False)
    D, N, SD, TO, NP, NG = c.D, c.N, c.SD, c.TO, c.NP, c.NG

    dr = lambda name, shape, dt, kind="Internal": nc.dram_tensor(name, list(shape), dt, kind=kind)
    xb = dr("xb", (N, D), F32, "ExternalInput")
    xo = dr("xo", (TO * 128, D), F32, "ExternalInput")
    xh = dr("xh", (128, D), F32, "ExternalInput")
    wq_d = dr("wq", (D, D), F32, "ExternalInput")
    wk_d = dr("wk", (D, D), F32, "ExternalInput")
    wv_d = dr("wv", (D, D), F32, "ExternalInput")
    bq_d = dr("bq", (D,), F32, "ExternalInput")
    bk_d = dr("bk", (D,), F32, "ExternalInput")
    bv_d = dr("bv", (D,), F32, "ExternalInput")
    wm1_d = dr("wm1", (2 * D, D), F32, "ExternalInput")
    bm1_d = dr("bm1", (D,), F32, "ExternalInput")
    wm2_d = dr("wm2", (D, D), F32, "ExternalInput")
    bm2_d = dr("bm2", (D,), F32, "ExternalInput")
    wo_d = dr("wo", (2 * D, D), F32, "ExternalInput")
    bo_d = dr("bo", (D,), F32, "ExternalInput")
    fw8_d = dr("fw8", (D, D), U8, "ExternalInput")
    fws_d = dr("fws", (128, 1), F32, "ExternalInput")
    fb_d = dr("fb", (D,), F32, "ExternalInput")
    dmask_d = dr("dmask", (128, 512), F32, "ExternalInput")
    out_d = dr("out", (TO * 128, D), F32, "ExternalOutput")
    xn_scr = dr("xn_scr", (N, D), BF16)
    k_scr = dr("k_scr", (NP, 128, SD * 512), BF16)
    vfm_scr = dr("vfm_scr", (NP, D, 512), BF16)
    v_scr = dr("v_scr", (N, D), BF16)
    xno_scr = dr("xno_scr", (TO * 128, D), BF16)
    a_scr = dr("a_scr", (TO * 128, D), BF16)
    xh_scr = dr("xh_scr", (128, D), BF16)
    q_scr = dr("q_scr", (NG, 128, SD * 512), BF16)
    w2t_scr = dr("w2t_scr", (D, D), BF16)
    bmt_scr = dr("bmt_scr", (D, D), BF16)
    part_scr = dr("part_scr", (NG, 128, SD * 512), BF16)
    glob_scr = dr("glob_scr", (TO * 128, D), BF16)
    res_scr = dr("res_scr", (NG, D, 512), BF16)
    wpre = dr("wpre", (8, 128, SD * D), BF16)
    IWQ, IWK, IWV, IA1, IB1, IO2, IWC, IBM = range(8)

    QSC = float(1.0 / np.sqrt(D))
    PS = ExitStack()
    with PS:
        sbp = lambda n, sh, dt: PS.enter_context(nc.sbuf_tensor(n, list(sh), dt))
        sem = lambda n: PS.enter_context(nc.semaphore(n))

        bias = sbp("bias", (128, 64), F32)
        BQ, BK, BV, BM1, BOB, BO, FB, BM2 = range(8)
        bcv = lambda slot, o: bias[:, 8 * slot + o:8 * slot + o + 1]
        dmask = sbp("dmask_sb", (128, 512), F32)
        fws_sb = sbp("fws_sb", (128, 1), F32)
        whalo = sbp("whalo", (128, SD * 128), F32)
        kcn = sbp("kcn", (128, 16), F32)
        KEPS, KQRT, K32, K12, KZ = 0, 1, 2, 3, 15
        kc = lambda i: kcn[:, i:i + 1]

        ch_v = Chain(nc, PS, "ch_v")
        ch_s = Chain(nc, PS, "ch_s")
        ch_g = Chain(nc, PS, "ch_g")
        s_bar = sem("s_bar")
        bar = [0]

        def barrier(block, sync_waits=(), pe_wait=None, clear=()):
            base = bar[0]
            bar[0] += 6 if clear else 5

            @block.sync
            def _(eng):
                for (s, v) in sync_waits:
                    eng.wait_ge(s, v)
                eng.sem_inc(s_bar, 1)
                eng.wait_ge(s_bar, base + 5)
                if clear:
                    for h in clear:
                        eng.sem_clear(h)
                    eng.sem_inc(s_bar, 1)

            @block.scalar
            def _(eng):
                ch_s.signal(eng, s_bar, 1)
                eng.wait_ge(s_bar, base + (6 if clear else 5))

            @block.vector
            def _(eng):
                ch_v.signal(eng, s_bar, 1)
                eng.wait_ge(s_bar, base + (6 if clear else 5))

            @block.gpsimd
            def _(eng):
                ch_g.signal(eng, s_bar, 1)
                eng.wait_ge(s_bar, base + (6 if clear else 5))

            @block.tensor
            def _(eng):
                i = eng.sem_inc(s_bar, 1)
                if pe_wait is not None:
                    i._wait_ge(pe_wait[0], pe_wait[1])
                eng.wait_ge(s_bar, base + (6 if clear else 5))

        def phase0():
            C0 = ExitStack()
            with C0:
                _lsems = []
                def sem(n, _st=C0, _ls=_lsems):
                    h = _st.enter_context(nc.semaphore(n))
                    _ls.append(h)
                    return h
                sb0 = lambda n, sh, dt: C0.enter_context(nc.sbuf_tensor(n, list(sh), dt))
                stg = [sb0("p0stg%d" % i, (128, D), F32) for i in range(2)]
                bcast = [sb0("p0bc%d" % i, (128, D), BF16) for i in range(2)]
                w_o1 = sb0("p0wo1", (128, SD * D), BF16)
                w_b1 = sb0("p0wb1", (128, SD * D), BF16)
                wm2tT = sb0("p0wm2tT", (128, SD * D), BF16)
                wbmT = sb0("p0wbmT", (128, SD * D), BF16)
                w_c0 = sb0("p0wc0", (128, SD * D), BF16)
                u8s = sb0("p0u8", (128, D), U8)
                mk8 = sb0("p0mk8", (128, D), U8)
                b28 = sb0("p0b28", (128, D), U8)
                b38 = sb0("p0b38", (128, D), U8)
                f32a = sb0("p0f32a", (128, D), F32)
                f32b = sb0("p0f32b", (128, D), F32)
                f32c = sb0("p0f32c", (128, D), F32)
                i32a = sb0("p0i32a", (128, D), I32)
                dect = sb0("p0dect", (128, D), BF16)
                xhx = sb0("p0xhx", (128, D), F32)
                xhsq = sb0("p0xhsq", (128, D), F32)
                xhb = sb0("p0xhb", (128, D), BF16)
                xhf = sb0("p0xhf", (128, SD * 128), BF16)
                bm2b = sb0("p0bm2b", (128, 8), BF16)
                ps0 = [C0.enter_context(nc.psum_tensor("p0ps%d" % i, [128, 512], F32)) for i in range(2)]
                psd = C0.enter_context(nc.psum_tensor("p0psd", [128, 8], F32))

                z_ld = [sem("z_ld0"), sem("z_ld1")]
                z_u8 = sem("z_u8")
                z_sm = sem("z_sm")
                z_wp = [sem("z_wp0"), sem("z_wp1")]
                z_wb = sem("z_wb")
                z_wb1 = sem("z_wb1")
                z_wxh = sem("z_wxh")
                z_wcm = sem("z_wcm")
                z_tm2 = sem("z_tm2")
                z_tbm = sem("z_tbm")
                z_txh = sem("z_txh")
                z_v = sem("z_v")
                z_s = sem("z_s")
                z_pe = sem("z_pe")
                z_ev = sem("z_ev")

                casts = []
                for wi, (src, r0) in [(IWQ, (wq_d, 0)), (IWK, (wk_d, 0)), (IWV, (wv_d, 0)),
                                      (IA1, (wm1_d, 0)), (IO2, (wo_d, D))]:
                    for i in range(SD):
                        casts.append((src, r0 + 128 * i, "w", (wi, i)))
                for i in range(SD):
                    casts.append((wm1_d, D + 128 * i, "k", (w_b1, i)))
                for i in range(SD):
                    casts.append((wo_d, 128 * i, "k", (w_o1, i)))
                for i in range(SD):
                    casts.append((wm2_d, 128 * i, "s", i))
                NCAST = len(casts)
                slot_wr = {0: 0, 1: 0}
                wr_prior = []
                nws = {0: 0, 1: 0}
                for n, (src, r0, kind, arg) in enumerate(casts):
                    if kind in ("w", "s"):
                        wr_prior.append(nws[n % 2])
                        nws[n % 2] += 1
                    else:
                        wr_prior.append(None)
                NWR0, NWR1 = nws[0], nws[1]
                NH = max(1, D // 512)
                WH = min(D, 512)
                NGC = SD * NH
                E_CAST = NCAST
                E_DEC = E_CAST + SD
                E_HALOV = E_DEC + 1
                E_HALOB = E_HALOV + 1
                E_BM2B = E_HALOB + 1

                block0 = C0.enter_context(nc.Block())

                @block0.vector
                def _(eng):
                    for (slot, val) in [(KEPS, 1e-5), (KQRT, 0.25), (K32, QSC), (K12, 1e-12),
                                        (KZ, 0.0)]:
                        ch_v(eng.memset(kc(slot), val))
                    for n, (src, r0, kind, arg) in enumerate(casts):
                        eng.wait_ge(z_ld[n % 2], 16 * (n // 2 + 1))
                        if kind in ("w", "s"):
                            if wr_prior[n]:
                                eng.wait_ge(z_wp[n % 2], 16 * wr_prior[n])
                            ch_v(eng.tensor_copy(bcast[n % 2][:], stg[n % 2][:]))
                        else:
                            dst, i = arg
                            ch_v(eng.tensor_copy(dst[:, D * i:D * i + D], stg[n % 2][:]))
                        ch_v.signal(eng, z_v, 1)
                    for i in range(SD):
                        eng.wait_ge(z_u8, 16 * (i + 1))
                        if i >= 1:
                            eng.wait_ge(z_wb, 16 * i)
                        u = u8s[:]
                        ch_v(eng.tensor_scalar(mk8[:], u, 0x78, None, op0=OP.bitwise_and))
                        ch_v(eng.tensor_scalar(mk8[:], mk8[:], 0x78, None, op0=OP.is_equal))
                        ch_v(eng.tensor_scalar(b28[:], u, 0xF7, None, op0=OP.bitwise_and))
                        ch_v(eng.tensor_copy(b38[:], u))
                        ch_v(eng.copy_predicated(b38[:], mk8[:], b28[:]))
                        ch_v(eng.tensor_copy(f32c[:], b38[:].bitcast(FP8)))
                        ch_v(eng.tensor_copy(f32a[:], mk8[:]))
                        ch_v(eng.tensor_scalar(f32a[:], f32a[:], 1.0, None, op0=OP.add))
                        ch_v(eng.tensor_tensor(dect[:], f32c[:], f32a[:], op=OP.mult))
                        ch_v.signal(eng, z_v, 1)
                    eng.wait_ge(z_sm, 16 * 10)
                    ch_v(eng.reduce_sum(kcn[:, 4:5], xhx[:], axis=AX))
                    ch_v(eng.tensor_scalar(kcn[:, 4:5], kcn[:, 4:5], -1.0 / D, None, op0=OP.mult))
                    ch_v(eng.tensor_scalar(xhx[:], xhx[:], kcn[:, 4:5], None, op0=OP.add))
                    ch_v(eng.tensor_tensor(xhsq[:], xhx[:], xhx[:], op=OP.mult))
                    ch_v(eng.reduce_sum(kcn[:, 5:6], xhsq[:], axis=AX))
                    ch_v(eng.tensor_scalar(kcn[:, 5:6], kcn[:, 5:6], 1.0 / D, None, op0=OP.mult))
                    ch_v.signal(eng, z_v, 1)
                    eng.wait_ge(z_s, 1)
                    ch_v(eng.reciprocal(kcn[:, 6:7], kcn[:, 7:8]))
                    ch_v(eng.tensor_scalar(xhb[:], xhx[:], kcn[:, 6:7], None, op0=OP.mult))
                    ch_v.signal(eng, z_v, 1)
                    ch_v(eng.tensor_scalar(bias[:, 8 * BQ:8 * BQ + SD], bias[:, 8 * BQ:8 * BQ + SD],
                                           QSC, None, op0=OP.mult))
                    ch_v(eng.tensor_copy(bm2b[:, 0:SD], bias[:, 8 * BM2:8 * BM2 + SD]))
                    ch_v.signal(eng, z_v, 1)
                    for g in range(NGC):
                        isub, jh = divmod(g, NH)
                        eng.wait_ge(z_pe, g + 1)
                        ch_v(eng.tensor_scalar(w_c0[:, D * isub + WH * jh:D * isub + WH * jh + WH],
                                               ps0[g % 2][:, 0:WH], kc(KQRT), None, op0=OP.mult))
                        ch_v.signal(eng, z_ev, 1)
                    for g in range(SD):
                        eng.wait_ge(z_pe, NGC + g + 1)
                        ch_v(eng.tensor_copy(whalo[:, 128 * g:128 * g + 128], ps0[g % 2][:, 0:128]))
                        ch_v.signal(eng, z_ev, 1)
                    eng.wait_ge(z_pe, NGC + SD + 1)
                    ch_v(eng.tensor_tensor(bias[:, 8 * BOB:8 * BOB + SD], psd[:, 0:SD],
                                           bias[:, 8 * BO:8 * BO + SD], op=OP.add))
                    ch_v(eng.tensor_tensor(bias[:, 8 * BOB:8 * BOB + SD], bias[:, 8 * BOB:8 * BOB + SD],
                                           bias[:, 8 * FB:8 * FB + SD], op=OP.add))
                    ch_v.signal(eng, z_ev, 1)

                @block0.sync
                def _(eng):
                    def bcast_write(n):
                        (src, r0, kind, arg) = casts[n]
                        if kind == "w":
                            wi, i = arg
                            eng.wait_ge(z_v, n + 1)
                            eng.dma_start(out=wpre[wi, :, D * i:D * i + D],
                                          in_=bcast[n % 2][:]).then_inc(z_wp[n % 2], 16)
                        elif kind == "s":
                            i = arg
                            eng.wait_ge(z_v, n + 1)
                            eng.dma_start(out=w2t_scr[128 * i:128 * i + 128, :],
                                          in_=bcast[n % 2][:]).then_inc(z_wp[n % 2], 16)

                    for n, (src, r0, kind, arg) in enumerate(casts):
                        if n >= 2:
                            eng.wait_ge(z_v, n - 1)
                        eng.dma_start(out=stg[n % 2][:], in_=src[r0:r0 + 128, :]).then_inc(z_ld[n % 2], 16)
                        if n >= 2:
                            bcast_write(n - 2)
                    bcast_write(NCAST - 2)
                    bcast_write(NCAST - 1)
                    for i in range(SD):
                        if i >= 1:
                            eng.wait_ge(z_v, E_CAST + i)
                        eng.dma_start(out=u8s[:], in_=fw8_d[128 * i:128 * i + 128, :]).then_inc(z_u8, 16)
                        if i >= 1:
                            eng.wait_ge(z_v, E_CAST + i)
                            eng.dma_start(out=bmt_scr[128 * (i - 1):128 * (i - 1) + 128, :],
                                          in_=dect[:]).then_inc(z_wb, 16)
                    eng.wait_ge(z_v, E_CAST + SD)
                    eng.dma_start(out=bmt_scr[128 * (SD - 1):128 * SD, :], in_=dect[:]).then_inc(z_wb, 16)
                    for (slot, src) in [(BQ, bq_d), (BK, bk_d), (BV, bv_d), (BM1, bm1_d),
                                        (BO, bo_d), (FB, fb_d), (BM2, bm2_d)]:
                        eng.dma_start(out=bias[:, 8 * slot:8 * slot + SD],
                                      in_=src[:].rearrange("(j p) -> p j", p=128),
                                      allow_slow_non_contiguous=True).then_inc(z_sm, 16)
                    eng.dma_start(out=dmask[:], in_=dmask_d[:, :]).then_inc(z_sm, 16)
                    eng.dma_start(out=fws_sb[:, 0:1], in_=fws_d[:, 0:1]).then_inc(z_sm, 16)
                    eng.dma_start(out=xhx[:], in_=xh[:, :]).then_inc(z_sm, 16)
                    eng.wait_ge(z_v, E_CAST)
                    eng.dma_start(out=wpre[IB1, :, :], in_=w_b1[:, :]).then_inc(z_wb1, 16)
                    eng.wait_ge(z_v, E_HALOB)
                    eng.dma_start(out=xh_scr[:, :], in_=xhb[:]).then_inc(z_wxh, 16)
                    eng.wait_ge(z_ev, NGC)
                    eng.dma_start(out=wpre[IWC, :, :], in_=w_c0[:, :]).then_inc(z_wcm, 16)
                    eng.wait_ge(z_tbm, 16 * SD)
                    eng.dma_start(out=wpre[IBM, :, :], in_=wbmT[:, :]).then_inc(z_wcm, 16)

                @block0.scalar
                def _(eng):
                    eng.wait_ge(z_v, E_HALOV)
                    ch_s(eng.activation(kcn[:, 7:8], kcn[:, 5:6], AF.Sqrt, bias=kc(KEPS)))
                    ch_s.signal(eng, z_s, 1)
                    eng.wait_ge(z_wp[0], 16 * NWR0)
                    eng.wait_ge(z_wp[1], 16 * NWR1)
                    for m in range(SD):
                        eng.dma_start_transpose(out=wm2tT[:, D * m:D * m + D],
                                                in_=w2t_scr[:, 128 * m:128 * m + 128]).then_inc(z_tm2, 16)
                    eng.wait_ge(z_wb, 16 * SD)
                    for k in range(SD):
                        eng.dma_start_transpose(out=wbmT[:, D * k:D * k + D],
                                                in_=bmt_scr[:, 128 * k:128 * k + 128]).then_inc(z_tbm, 16)
                    eng.wait_ge(z_wxh, 16)
                    for i in range(SD):
                        eng.dma_start_transpose(out=xhf[:, 128 * i:128 * i + 128],
                                                in_=xh_scr[:, 128 * i:128 * i + 128]).then_inc(z_txh, 16)

                @block0.tensor
                def _(eng):
                    eng.wait_ge(z_v, E_CAST)
                    eng.wait_ge(z_tm2, 16 * SD)
                    g = 0
                    for isub in range(SD):
                        for jh in range(NH):
                            if g >= 2:
                                eng.wait_ge(z_ev, g - 1)
                            for m in range(SD):
                                ins = eng.matmul(ps0[g % 2][:, 0:WH],
                                                 wm2tT[:, D * m + 128 * isub:D * m + 128 * isub + 128],
                                                 w_o1[:, D * m + WH * jh:D * m + WH * jh + WH],
                                                 start=(m == 0), stop=(m == SD - 1))
                                if m == SD - 1:
                                    ins.then_inc(z_pe, 1)
                            g += 1
                    eng.wait_ge(z_txh, 16 * SD)
                    for o in range(SD):
                        gg = NGC + o
                        eng.wait_ge(z_ev, gg - 1)
                        for i in range(SD):
                            ins = eng.matmul(ps0[gg % 2][:, 0:128],
                                             w_b1[:, D * i + 128 * o:D * i + 128 * o + 128],
                                             xhf[:, 128 * i:128 * i + 128],
                                             start=(i == 0), stop=(i == SD - 1))
                            if i == SD - 1:
                                ins.then_inc(z_pe, 1)
                    eng.wait_ge(z_v, E_BM2B)
                    eng.wait_ge(z_ev, NGC + SD)
                    for jsub in range(SD):
                        for i in range(SD):
                            ins = eng.matmul(psd[:, jsub:jsub + 1],
                                             w_o1[:, D * i + 128 * jsub:D * i + 128 * jsub + 128],
                                             bm2b[:, i:i + 1],
                                             start=(i == 0), stop=(i == SD - 1))
                            if i == SD - 1 and jsub == SD - 1:
                                ins.then_inc(z_pe, 1)

                barrier(block0,
                        sync_waits=[(z_wb1, 16), (z_wcm, 32), (z_ev, NGC + SD + 1)])

        def phase1a():
            C1 = ExitStack()
            with C1:
                _lsems = []
                def sem(n, _st=C1, _ls=_lsems):
                    h = _st.enter_context(nc.semaphore(n))
                    _ls.append(h)
                    return h
                sb1 = lambda n, sh, dt: C1.enter_context(nc.sbuf_tensor(n, list(sh), dt))
                w_k = sb1("w_k1", (128, SD * D), BF16)
                w_v = sb1("w_v1", (128, SD * D), BF16)
                xbuf = [sb1("xbuf%d" % i, (128, 4 * D), F32) for i in range(2)]
                xc32 = sb1("xc32", (128, 4 * D), F32)
                sq32 = sb1("sq32", (128, D), F32)
                st4 = [sb1("st4_%d" % i, (128, 16), F32) for i in range(2)]
                xnb = [sb1("xnb%d" % i, (128, 4 * D), BF16) for i in range(2)]
                xf = [sb1("xf%d" % i, (128, SD * 512), BF16) for i in range(2)]
                ksb = [sb1("ksb%d" % i, (128, SD * 512), BF16) for i in range(2)]
                vsb = [sb1("vsb%d" % i, (128, SD * 512), BF16) for i in range(2)]
                vtb = [sb1("vtb%d" % i, (128, 4 * D), BF16) for i in range(2)]
                xob = [sb1("xob%d" % i, (128, D), F32) for i in range(2)]
                xoc = sb1("xoc", (128, D), F32)
                ost = [sb1("ost%d" % i, (128, 8), F32) for i in range(2)]
                xnob = [sb1("xnob%d" % i, (128, D), BF16) for i in range(2)]
                yq = sb1("yq", (128, D), F32)
                q8t = sb1("q8t", (128, D), FP8)
                abuf = sb1("abuf", (128, D), BF16)
                aout = [sb1("aout%d" % i, (128, D), BF16) for i in range(2)]
                ps1 = [C1.enter_context(nc.psum_tensor("p1ps%d" % i, [128, 512], F32)) for i in range(6)]
                HB = SD // 2

                a_wl = sem("a_wl")
                a_xld = [sem("a_xld0"), sem("a_xld1")]
                a_old = [sem("a_old0"), sem("a_old1")]
                a_xnw = [sem("a_xnw0"), sem("a_xnw1")]
                a_txf = [sem("a_txf0"), sem("a_txf1")]
                a_txv = [sem("a_txv0"), sem("a_txv1")]
                a_var = sem("a_var")
                a_sq = sem("a_sq")
                a_xnr = sem("a_xnr")
                a_kst = sem("a_kst")
                a_vst = sem("a_vst")
                a_pe = sem("a_pe")
                a_ev = sem("a_ev")
                a_kw = [sem("a_kw0"), sem("a_kw1")]
                a_vfw = [sem("a_vfw0"), sem("a_vfw1")]
                a_vtw = [sem("a_vtw0"), sem("a_vtw1")]
                a_onw = [sem("a_onw0"), sem("a_onw1")]
                a_aw = [sem("a_aw0"), sem("a_aw1")]
                a_vch = sem("a_vch")
                a_och = sem("a_och")

                block1 = C1.enter_context(nc.Block())

                @block1.sync
                def _(eng):
                    eng.dma_start(out=w_k[:], in_=wpre[IWK, :, :]).then_inc(a_wl, 16)
                    eng.dma_start(out=w_v[:], in_=wpre[IWV, :, :]).then_inc(a_wl, 16)

                    def load_x(q):
                        eng.dma_start(out=xbuf[q % 2][:].rearrange("p (t d) -> p t d", d=D),
                                      in_=xb[512 * q:512 * q + 512, :].rearrange("(t p) d -> p t d", p=128)) \
                           .then_inc(a_xld[q % 2], 16)

                    def load_xo(q):
                        eng.dma_start(out=xob[q % 2][:], in_=xo[128 * q:128 * q + 128, :]) \
                           .then_inc(a_old[q % 2], 16)

                    def kv_writes(q):
                        sq = q % 2
                        eng.wait_ge(a_kst, q + 1)
                        eng.dma_start(out=k_scr[q, :, :], in_=ksb[sq][:]).then_inc(a_kw[sq], 16)
                        eng.wait_ge(a_vst, q + 1)
                        for o in range(SD):
                            eng.dma_start(out=vfm_scr[q, 128 * o:128 * o + 128, :],
                                          in_=vsb[sq][:, 512 * o:512 * o + 512]).then_inc(a_vfw[sq], 16)

                    def vtb_chain(q):
                        sq = q % 2
                        if q >= 2:
                            eng.wait_ge(a_vtw[sq], 16 * (q // 2))
                        eng.wait_ge(a_vfw[sq], 16 * SD * (q // 2 + 1))
                        for t in range(4):
                            eng.dma_start_transpose(out=vtb[sq][:, D * t:D * t + D],
                                                    in_=vfm_scr[q, :, 128 * t:128 * t + 128]) \
                               .then_inc(a_txv[sq], 16)
                        eng.wait_ge(a_txv[sq], 16 * 4 * (q // 2 + 1))
                        eng.dma_start(out=v_scr[512 * q:512 * q + 512, :].rearrange("(t p) d -> p t d", p=128),
                                      in_=vtb[sq][:].rearrange("p (t d) -> p t d", d=D)) \
                           .then_inc(a_vtw[sq], 16)

                    for q in range(min(2, NP)):
                        load_x(q)
                        load_xo(q)
                    for p in range(NP):
                        s = p % 2
                        if p + 2 < NP:
                            eng.wait_ge(a_vch, p + 1)
                            load_x(p + 2)
                            eng.wait_ge(a_och, p + 1)
                            load_xo(p + 2)
                        eng.wait_ge(a_xnr, 3 * p + 1)
                        eng.dma_start(out=xn_scr[512 * p:512 * p + 512, :].rearrange("(t p) d -> p t d", p=128),
                                      in_=xnb[s][:].rearrange("p (t d) -> p t d", d=D)) \
                           .then_inc(a_xnw[s], 16)
                        eng.wait_ge(a_xnr, 3 * p + 2)
                        eng.dma_start(out=xno_scr[128 * p:128 * p + 128, :], in_=xnob[s][:]).then_inc(a_onw[s], 16)
                        eng.wait_ge(a_xnr, 3 * p + 3)
                        eng.dma_start(out=a_scr[128 * p:128 * p + 128, :], in_=aout[s][:]).then_inc(a_aw[s], 16)
                        if p >= 1:
                            kv_writes(p - 1)
                        if p >= 2:
                            vtb_chain(p - 2)
                    kv_writes(NP - 1)
                    for q in (NP - 2, NP - 1):
                        if q >= 0:
                            vtb_chain(q)

                @block1.vector
                def _(eng):
                    for p in range(NP):
                        s = p % 2
                        eng.wait_ge(a_xld[s], 16 * (p // 2 + 1))
                        x4 = xbuf[s][:].rearrange("p (t d) -> p t d", d=D)
                        ch_v(eng.reduce_sum(st4[s][:, 0:4].rearrange("p (t f) -> p t f", f=1), x4, axis=AX))
                        ch_v(eng.tensor_scalar(st4[s][:, 0:4], st4[s][:, 0:4], -1.0 / D, None, op0=OP.mult))
                        for t in range(4):
                            ch_v(eng.tensor_scalar(xc32[:, D * t:D * t + D], xbuf[s][:, D * t:D * t + D],
                                                   st4[s][:, t:t + 1], None, op0=OP.add))
                        ch_v(eng.tensor_tensor(xbuf[s][:], xc32[:], xc32[:], op=OP.mult))
                        ch_v(eng.reduce_sum(st4[s][:, 4:8].rearrange("p (t f) -> p t f", f=1),
                                            xbuf[s][:].rearrange("p (t d) -> p t d", d=D), axis=AX))
                        ch_v.signal(eng, a_vch, 1)
                        ch_v(eng.tensor_scalar(st4[s][:, 4:8], st4[s][:, 4:8], 1.0 / D, None, op0=OP.mult))
                        ch_v.signal(eng, a_var, 1)
                        eng.wait_ge(a_sq, 2 * p + 1)
                        if p >= 2:
                            eng.wait_ge(a_xnw[s], 16 * (p // 2))
                        ch_v(eng.reciprocal(st4[s][:, 8:12], st4[s][:, 12:16]))
                        for t in range(4):
                            ch_v(eng.tensor_scalar(xnb[s][:, D * t:D * t + D], xc32[:, D * t:D * t + D],
                                                   st4[s][:, 8 + t:9 + t], None, op0=OP.mult))
                        ch_v.signal(eng, a_xnr, 1)
                        eng.wait_ge(a_old[s], 16 * (p // 2 + 1))
                        ch_v(eng.reduce_sum(ost[s][:, 0:1], xob[s][:], axis=AX))
                        ch_v(eng.tensor_scalar(ost[s][:, 0:1], ost[s][:, 0:1], -1.0 / D, None, op0=OP.mult))
                        ch_v(eng.tensor_scalar(xoc[:], xob[s][:], ost[s][:, 0:1], None, op0=OP.add))
                        ch_v.signal(eng, a_och, 1)
                        ch_v(eng.tensor_tensor(sq32[:], xoc[:], xoc[:], op=OP.mult))
                        ch_v(eng.reduce_sum(ost[s][:, 1:2], sq32[:], axis=AX))
                        ch_v(eng.tensor_scalar(ost[s][:, 1:2], ost[s][:, 1:2], 1.0 / D, None, op0=OP.mult))
                        ch_v.signal(eng, a_var, 1)
                        eng.wait_ge(a_sq, 2 * p + 2)
                        if p >= 2:
                            eng.wait_ge(a_onw[s], 16 * (p // 2))
                            eng.wait_ge(a_aw[s], 16 * (p // 2))
                        ch_v(eng.reciprocal(ost[s][:, 2:3], ost[s][:, 3:4]))
                        ch_v(eng.tensor_scalar(xoc[:], xoc[:], ost[s][:, 2:3], None, op0=OP.mult))
                        ch_v(eng.tensor_copy(xnob[s][:], xoc[:]))
                        ch_v.signal(eng, a_xnr, 1)
                        ch_v(eng.tensor_reduce(ost[s][:, 4:5], xoc[:], axis=AX, op=OP.max,
                                               apply_absolute_value=True))
                        ch_v(eng.tensor_scalar(ost[s][:, 4:5], ost[s][:, 4:5], kc(K12), None, op0=OP.max))
                        ch_v(eng.reciprocal(ost[s][:, 5:6], ost[s][:, 4:5]))
                        ch_v(eng.tensor_scalar(ost[s][:, 5:6], ost[s][:, 5:6], 448.0, None, op0=OP.mult))
                        ch_v(eng.tensor_scalar(yq[:], xoc[:], ost[s][:, 5:6], 0.5, op0=OP.mult, op1=OP.mult))
                        ch_v(eng.tensor_copy(q8t[:], yq[:]))
                        ch_v(eng.reciprocal(ost[s][:, 6:7], ost[s][:, 5:6]))
                        ch_v(eng.tensor_scalar(ost[s][:, 6:7], ost[s][:, 6:7], 2.0, fws_sb[:, 0:1],
                                               op0=OP.mult, op1=OP.mult))
                        ch_v(eng.tensor_scalar(abuf[:], q8t[:], ost[s][:, 6:7], None, op0=OP.mult))
                        ch_v(eng.tensor_copy(aout[s][:].bitcast(U8), abuf[:].bitcast(U8)))
                        ch_v.signal(eng, a_xnr, 1)

                @block1.scalar
                def _(eng):
                    def sqrt_main(q):
                        eng.wait_ge(a_var, 2 * q + 1)
                        ch_s(eng.activation(st4[q % 2][:, 12:16], st4[q % 2][:, 4:8], AF.Sqrt, bias=kc(KEPS)))
                        ch_s.signal(eng, a_sq, 1)

                    def sqrt_own(q):
                        eng.wait_ge(a_var, 2 * q + 2)
                        ch_s(eng.activation(ost[q % 2][:, 3:4], ost[q % 2][:, 1:2], AF.Sqrt, bias=kc(KEPS)))
                        ch_s.signal(eng, a_sq, 1)

                    def transposes(q):
                        sq = q % 2
                        if q >= 2:
                            eng.wait_ge(a_pe, 2 * SD * (q - 1))
                        eng.wait_ge(a_xnw[sq], 16 * (q // 2 + 1))
                        for i in range(SD):
                            eng.dma_start_transpose(out=xf[sq][:, 512 * i:512 * i + 512],
                                                    in_=xn_scr[512 * q:512 * q + 512, 128 * i:128 * i + 128]) \
                               .then_inc(a_txf[sq], 16)

                    POS_OWN = min(3, SD - 1)
                    POS_TR = min(5, SD - 1)
                    sqrt_main(0)
                    sqrt_own(0)
                    transposes(0)
                    for p in range(NP):
                        s = p % 2
                        if p >= 2:
                            eng.wait_ge(a_kw[s], 16 * (p // 2))
                        for o in range(SD):
                            G = 2 * SD * p + o
                            eng.wait_ge(a_pe, G + 1)
                            ch_s(eng.activation(ksb[s][:, 512 * o:512 * o + 512],
                                                ps1[G % 6][:, :],
                                                AF.Identity, bias=bcv(BK, o)))
                            ch_s.signal(eng, a_ev, 1)
                        ch_s.signal(eng, a_kst, 1)
                        if p + 1 < NP:
                            sqrt_main(p + 1)
                        if p >= 2:
                            eng.wait_ge(a_vfw[s], 16 * SD * (p // 2))
                        for o in range(SD):
                            G = 2 * SD * p + SD + o
                            eng.wait_ge(a_pe, G + 1)
                            ch_s(eng.activation(vsb[s][:, 512 * o:512 * o + 512],
                                                ps1[G % 6][:, :],
                                                AF.Identity, bias=bcv(BV, o)))
                            ch_s.signal(eng, a_ev, 1)
                            if o == POS_OWN and p + 1 < NP:
                                sqrt_own(p + 1)
                            if o == POS_TR and p + 1 < NP:
                                transposes(p + 1)
                        ch_s.signal(eng, a_vst, 1)

                @block1.tensor
                def _(eng):
                    eng.wait_ge(a_wl, 32)
                    for p in range(NP):
                        s = p % 2
                        eng.wait_ge(a_txf[s], 16 * SD * (p // 2 + 1))
                        for mat in range(2):
                            w = w_k if mat == 0 else w_v
                            for o in range(SD):
                                G = 2 * SD * p + SD * mat + o
                                if G >= 6:
                                    eng.wait_ge(a_ev, G - 5)
                                dst = ps1[G % 6][:, :]
                                for i in range(SD):
                                    ins = eng.matmul(dst,
                                                     w[:, D * i + 128 * o:D * i + 128 * o + 128],
                                                     xf[s][:, 512 * i:512 * i + 512],
                                                     start=(i == 0), stop=(i == SD - 1))
                                    if i == SD - 1:
                                        ins.then_inc(a_pe, 1)

                barrier(block1, sync_waits=[
                    (a_kw[0], 16 * (NP // 2)), (a_kw[1], 16 * (NP // 2)),
                    (a_vtw[0], 16 * (NP // 2)), (a_vtw[1], 16 * (NP // 2)),
                    (a_onw[0], 16 * (NP // 2)), (a_onw[1], 16 * (NP // 2)),
                    (a_aw[0], 16 * (NP // 2)), (a_aw[1], 16 * (NP // 2)),
                    (a_vfw[0], 16 * SD * (NP // 2)), (a_vfw[1], 16 * SD * (NP // 2))])

        def phase1b():
            C2 = ExitStack()
            with C2:
                _lsems = []
                def sem(n, _st=C2, _ls=_lsems):
                    h = _st.enter_context(nc.semaphore(n))
                    _ls.append(h)
                    return h
                sb2 = lambda n, sh, dt: C2.enter_context(nc.sbuf_tensor(n, list(sh), dt))
                w_q = sb2("w_q2", (128, SD * D), BF16)
                w_a1 = sb2("w_a12", (128, SD * D), BF16)
                w_b1 = sb2("w_b12", (128, SD * D), BF16)
                w_c = sb2("w_c2", (128, SD * D), BF16)
                w_bm = sb2("w_bm2", (128, SD * D), BF16)
                xof_ = sb2("xof", (128, SD * 512), BF16)
                af_ = sb2("af", (128, SD * 512), BF16)
                qsb_ = sb2("qsb", (128, SD * 512), BF16)
                ub = sb2("ub", (128, SD * 512), BF16)
                wcat = sb2("wcat", (128, SD * 528), BF16)
                gsum = sb2("gsum", (128, SD * 512), F32)
                zt = [sb2("zt%d" % i, (128, SD * 512), BF16) for i in range(2)]
                su = [sb2("su%d" % i, (128, SD * 512), BF16) for i in range(2)]
                gb = sb2("gb", (128, SD * 512), BF16)
                prt_ = sb2("prt", (128, SD * 512), BF16)
                ps2 = [C2.enter_context(nc.psum_tensor("p2ps%d" % i, [128, 512], F32)) for i in range(6)]

                b_wl = sem("b_wl")
                b_txx = sem("b_txx")
                b_ta = sem("b_ta")
                b_pe = sem("b_pe")
                b_ev = sem("b_ev")
                b_z = sem("b_z")
                b_sil = sem("b_sil")
                b_gad = sem("b_gad")
                b_gb = sem("b_gb")
                b_qs = sem("b_qs")
                b_qw = sem("b_qw")
                b_pw = sem("b_pw")

                GPE = 4 * SD
                HF = SD // 2
                HW = HF * 512
                zix = lambda g, w, h: 8 * g + (w - 1) + 4 * h + 1
                aix = lambda g, w, h: 6 * g + (w - 2) + 3 * h + 1
                block2 = C2.enter_context(nc.Block())

                @block2.sync
                def _(eng):
                    for (w, wi) in [(w_q, IWQ), (w_a1, IA1), (w_b1, IB1), (w_c, IWC), (w_bm, IBM)]:
                        eng.dma_start(out=w[:], in_=wpre[wi, :, :]).then_inc(b_wl, 16)
                    for g in range(NG):
                        eng.wait_ge(b_qs, 2 * g + 1)
                        eng.dma_start(out=q_scr[g, :, :], in_=qsb_[:]).then_inc(b_qw, 16)
                        eng.wait_ge(b_qs, 2 * g + 2)
                        eng.dma_start(out=part_scr[g, :, :], in_=prt_[:]).then_inc(b_pw, 16)

                @block2.scalar
                def _(eng):
                    def silu(g, w, h):
                        eng.wait_ge(b_z, zix(g, w, h))
                        if w == 1:
                            if g >= 1:
                                eng.wait_ge(b_gb, 2 * g)
                            ch_s(eng.activation(gsum[:, HW * h:HW * h + HW],
                                                zt[1][:, HW * h:HW * h + HW], AF.Silu))
                        else:
                            if w == 4:
                                eng.wait_ge(b_gad, aix(g, 2, h))
                            elif g >= 1:
                                eng.wait_ge(b_gad, aix(g - 1, 6 - w, h))
                            ch_s(eng.activation(su[w % 2][:, HW * h:HW * h + HW],
                                                zt[w % 2][:, HW * h:HW * h + HW], AF.Silu))
                        ch_s.signal(eng, b_sil, 1)

                    for g in range(NG):
                        if g >= 1:
                            eng.wait_ge(b_pe, GPE * (g - 1) + 3 * SD)
                        for i in range(SD):
                            eng.dma_start_transpose(out=xof_[:, 512 * i:512 * i + 512],
                                                    in_=xno_scr[512 * g:512 * g + 512, 128 * i:128 * i + 128]) \
                               .then_inc(b_txx, 16)
                        if g >= 1:
                            eng.wait_ge(b_pe, GPE * g)
                        for i in range(SD):
                            eng.dma_start_transpose(out=af_[:, 512 * i:512 * i + 512],
                                                    in_=a_scr[512 * g:512 * g + 512, 128 * i:128 * i + 128]) \
                               .then_inc(b_ta, 16)
                        for o in range(SD):
                            gg = GPE * g + o
                            eng.wait_ge(b_pe, gg + 1)
                            ch_s(eng.activation(ub[:, 512 * o:512 * o + 512], ps2[gg % 6][:, :],
                                                AF.Identity, bias=bcv(BM1, o)))
                            ch_s.signal(eng, b_ev, 1)
                        for o in range(SD):
                            gg = GPE * g + SD + o
                            eng.wait_ge(b_pe, gg + 1)
                            ch_s(eng.activation(wcat[:, 528 * o:528 * o + 528]
                                                .rearrange("p (j n) -> p j n", n=132)[:, :, 4:132],
                                                ps2[gg % 6][:, :].rearrange("p (j n) -> p j n", n=128),
                                                AF.Copy))
                            ch_s.signal(eng, b_ev, 1)
                        if g >= 1:
                            eng.wait_ge(b_qw, 16 * g)

                        def qevac(o):
                            gg = GPE * g + 2 * SD + o
                            eng.wait_ge(b_pe, gg + 1)
                            ch_s(eng.activation(qsb_[:, 512 * o:512 * o + 512], ps2[gg % 6][:, :],
                                                AF.Identity, bias=bcv(BQ, o), scale=kc(K32)))
                            ch_s.signal(eng, b_ev, 1)

                        qi = 0
                        for si, (w, h) in enumerate([(1, 0), (2, 0), (3, 0), (4, 0),
                                                     (1, 1), (2, 1), (3, 1), (4, 1)]):
                            silu(g, w, h)
                            if si >= 1 and qi < SD:
                                qevac(qi)
                                qi += 1
                        while qi < SD:
                            qevac(qi)
                            qi += 1
                        ch_s.signal(eng, b_qs, 1)
                        if g >= 1:
                            eng.wait_ge(b_pw, 16 * g)
                        for o in range(SD):
                            gg = GPE * g + 3 * SD + o
                            eng.wait_ge(b_pe, gg + 1)
                            ch_s(eng.activation(prt_[:, 512 * o:512 * o + 512], ps2[gg % 6][:, :],
                                                AF.Identity, bias=bcv(BOB, o)))
                            ch_s.signal(eng, b_ev, 1)
                        ch_s.signal(eng, b_qs, 1)

                @block2.vector
                def _(eng):
                    r4 = lambda t: t[:].rearrange("p (o j n) -> p o j n", j=4, n=128)
                    rw = lambda: wcat[:].rearrange("p (o j n) -> p o j n", j=4, n=132)

                    for g in range(NG):
                        ch_v(eng.tensor_copy(
                            rw()[:, :, :, 0:4],
                            whalo[:].rearrange("p (o t k) -> p o t k", t=32, k=4)[:, :, 4 * g:4 * g + 4, :]))

                        def zop(w, h):
                            if w == 1 and g >= 1:
                                eng.wait_ge(b_sil, zix(g - 1, 3, h))
                            if w == 2 and g >= 1:
                                eng.wait_ge(b_sil, zix(g - 1, 4, h))
                            if w == 3:
                                eng.wait_ge(b_sil, zix(g, 1, h))
                            if w == 4:
                                eng.wait_ge(b_sil, zix(g, 2, h))
                            if w == 1 and h == 0:
                                eng.wait_ge(b_ev, GPE * g + SD + HF)
                            if w == 1 and h == 1:
                                eng.wait_ge(b_ev, GPE * g + 2 * SD)
                            ch_v(eng.tensor_tensor(
                                r4(zt[w % 2])[:, HF * h:HF * h + HF],
                                r4(ub)[:, HF * h:HF * h + HF],
                                rw()[:, HF * h:HF * h + HF, :, 4 - w:132 - w],
                                op=OP.add))
                            ch_v.signal(eng, b_z, 1)

                        def gadd(w, h):
                            eng.wait_ge(b_sil, zix(g, w, h))
                            ch_v(eng.tensor_tensor(gsum[:, HW * h:HW * h + HW],
                                                   gsum[:, HW * h:HW * h + HW],
                                                   su[w % 2][:, HW * h:HW * h + HW], op=OP.add))
                            ch_v.signal(eng, b_gad, 1)

                        def gbcopy(h):
                            ch_v(eng.tensor_copy(gb[:, HW * h:HW * h + HW],
                                                 gsum[:, HW * h:HW * h + HW]))
                            ch_v.signal(eng, b_gb, 1)

                        zop(1, 0); zop(2, 0); zop(3, 0); zop(4, 0)
                        zop(1, 1); zop(2, 1)
                        if g >= 1:
                            eng.wait_ge(b_pe, GPE * (g - 1) + 4 * SD)
                        gadd(2, 0); gadd(3, 0); gadd(4, 0); gbcopy(0)
                        zop(3, 1); zop(4, 1)
                        gadd(2, 1); gadd(3, 1); gadd(4, 1); gbcopy(1)

                @block2.tensor
                def _(eng):
                    eng.wait_ge(b_wl, 16 * 5)
                    for g in range(NG):
                        eng.wait_ge(b_txx, 16 * SD * (g + 1))
                        for mat, w in ((0, w_a1), (1, w_b1), (2, w_q)):
                            for o in range(SD):
                                gg = GPE * g + SD * mat + o
                                if gg >= 6:
                                    eng.wait_ge(b_ev, gg - 5)
                                for i in range(SD):
                                    ins = eng.matmul(ps2[gg % 6][:, :],
                                                     w[:, D * i + 128 * o:D * i + 128 * o + 128],
                                                     xof_[:, 512 * i:512 * i + 512],
                                                     start=(i == 0), stop=(i == SD - 1))
                                    if i == SD - 1:
                                        ins.then_inc(b_pe, 1)
                        eng.wait_ge(b_gb, 2 * (g + 1))
                        eng.wait_ge(b_ta, 16 * SD * (g + 1))
                        for o in range(SD):
                            gg = GPE * g + 3 * SD + o
                            if gg >= 6:
                                eng.wait_ge(b_ev, gg - 5)
                            for i in range(SD):
                                eng.matmul(ps2[gg % 6][:, :],
                                           w_c[:, D * i + 128 * o:D * i + 128 * o + 128],
                                           gb[:, 512 * i:512 * i + 512],
                                           start=(i == 0), stop=False)
                            for i in range(SD):
                                ins = eng.matmul(ps2[gg % 6][:, :],
                                                 w_bm[:, D * i + 128 * o:D * i + 128 * o + 128],
                                                 af_[:, 512 * i:512 * i + 512],
                                                 start=False, stop=(i == SD - 1))
                                if i == SD - 1:
                                    ins.then_inc(b_pe, 1)

                barrier(block2, sync_waits=[(b_qw, 16 * NG), (b_pw, 16 * NG)])

        def phase2():
            C3 = ExitStack()
            with C3:
                _lsems = []
                def sem(n, _st=C3, _ls=_lsems):
                    h = _st.enter_context(nc.semaphore(n))
                    _ls.append(h)
                    return h
                sb3 = lambda n, sh, dt: C3.enter_context(nc.sbuf_tensor(n, list(sh), dt))
                qb = sb3("qb", (128, NG * SD * 512), BF16)
                kp = [sb3("kp%d" % i, (128, SD * 512), BF16) for i in range(3)]
                vb = sb3("vb", (128, 4 * TO * (TO + 1)), F32)
                ib = sb3("ib", (128, 4 * TO * (TO + 1)), F32)
                off = [4 * t * (t + 1) for t in range(TO + 1)]
                u32t = sb3("u32t", (128, 8), U32)
                f8 = sb3("f8", (128, 8), F32)
                e8 = sb3("e8", (128, 8), F32)
                p8 = [sb3("p8_%d" % i, (128, 8), F32) for i in range(2)]
                negm = sb3("negm", (128, 1), F32)
                lsum = sb3("lsum", (128, 1), F32)
                tmp8 = sb3("tmp8", (128, 8 * TO), F32)
                idm = sb3("idm", (128, 8), F32)
                idxi = sb3("idxi", (128, 8), I32)
                vsel = [sb3("vsel%d" % i, (128, 8 * D), BF16) for i in range(2)]
                gacc = [sb3("gacc%d" % i, (128, D), F32) for i in range(2)]
                gbf = [sb3("gbf%d" % i, (128, D), BF16) for i in range(2)]
                wtm = [sb3("wtm%d" % j, (128, D), BF16) for j in range(8)]
                ps3 = [C3.enter_context(nc.psum_tensor("p3ps%d" % i, [128, 512], F32)) for i in range(4)]

                c_ql = sem("c_ql")
                c_kp = [sem("c_kp0"), sem("c_kp1"), sem("c_kp2")]
                c_kf = sem("c_kf")
                c_un = sem("c_un")
                c_uv = sem("c_uv")
                c_f8 = sem("c_f8")
                c_ex = sem("c_ex")
                c_gt = sem("c_gt")
                c_gg = sem("c_gg")
                c_gw = [sem("c_gw0"), sem("c_gw1")]
                c_wm = sem("c_wm")
                c_wa = sem("c_wa")

                units = []
                for p in range(NP):
                    for j in range(p, TO):
                        units.append((p, j))
                uid = {pj: i for i, pj in enumerate(units)}
                NU = len(units)

                block3 = C3.enter_context(nc.Block())

                @block3.sync
                def _(eng):
                    for g in range(NG):
                        eng.dma_start(out=qb[:, SD * 512 * g:SD * 512 * (g + 1)],
                                      in_=q_scr[g, :, :]).then_inc(c_ql, 16)
                    def glob_write(t):
                        eng.wait_ge(c_gg, t + 1)
                        eng.dma_start(out=glob_scr[128 * t:128 * t + 128, :], in_=gbf[t % 2][:]) \
                           .then_inc(c_gw[t % 2], 16)

                    for p in range(NP):
                        if p >= 3:
                            eng.wait_ge(c_kf, p - 2)
                        eng.dma_start(out=kp[p % 3][:], in_=k_scr[p, :, :]).then_inc(c_kp[p % 3], 16)
                        if p >= 3:
                            glob_write(p - 3)
                    for t in range(max(0, TO - 3), TO):
                        glob_write(t)

                @block3.tensor
                def _(eng):
                    eng.wait_ge(c_ql, 16 * NG)
                    for p in range(NP):
                        eng.wait_ge(c_kp[p % 3], 16 * (p // 3 + 1))
                        for j in range(p, TO):
                            u = uid[(p, j)]
                            if u >= 4:
                                eng.wait_ge(c_uv, u - 3)
                            g4, j4 = divmod(j, 4)
                            for i in range(SD):
                                ins = eng.matmul(ps3[u % 4][:, :],
                                                 qb[:, SD * 512 * g4 + 512 * i + 128 * j4:
                                                     SD * 512 * g4 + 512 * i + 128 * j4 + 128],
                                                 kp[p % 3][:, 512 * i:512 * i + 512],
                                                 start=(i == 0), stop=(i == SD - 1))
                                if i == SD - 1:
                                    ins.then_inc(c_un, 1)
                        i = eng.sem_inc(c_kf, 1)
                        i._wait_ge(c_un, uid[(p, TO - 1)] + 1)

                @block3.vector
                def _(eng):
                    for p in range(NP):
                        for j in range(p, TO):
                            u = uid[(p, j)]
                            eng.wait_ge(c_un, u + 1)
                            if j == p:
                                ch_v(eng.tensor_tensor(ps3[u % 4][:, :], ps3[u % 4][:, :],
                                                       dmask[:], op=OP.add))
                            ch_v(eng.max(out=vb[:, off[j] + 8 * p:off[j] + 8 * p + 8],
                                         in_=ps3[u % 4][:, :]))
                            ch_v(eng.max_index(out=u32t[:], in_max=vb[:, off[j] + 8 * p:off[j] + 8 * p + 8],
                                               in_values=ps3[u % 4][:, :]))
                            ch_v(eng.tensor_scalar(ib[:, off[j] + 8 * p:off[j] + 8 * p + 8],
                                                   u32t[:], float(512 * p - BIGI), None, op0=OP.add))
                            ch_v.signal(eng, c_uv, 1)
                        t = p
                        if t < TO:
                            W8 = 8 * (t + 1)
                            eng.wait_ge(c_gt, 128 * t)
                            ch_v(eng.max(out=f8[:], in_=vb[:, off[t]:off[t] + W8]))
                            ch_v(eng.tensor_scalar(negm[:, 0:1], f8[:, 0:1], -1.0, None, op0=OP.mult))
                            ch_v.signal(eng, c_f8, 1)
                            eng.wait_ge(c_ex, t + 1)
                            ch_v(eng.reduce_sum(lsum[:, 0:1], e8[:], axis=AX))
                            ch_v(eng.reciprocal(lsum[:, 0:1], lsum[:, 0:1]))
                            ch_v(eng.tensor_scalar(p8[t % 2][:], e8[:], lsum[:, 0:1], None, op0=OP.mult))
                            for k in range(8):
                                ch_v(eng.scalar_tensor_tensor(tmp8[:, 0:W8], vb[:, off[t]:off[t] + W8],
                                                              f8[:, k:k + 1], ib[:, off[t]:off[t] + W8],
                                                              op0=OP.is_equal, op1=OP.mult))
                                ch_v(eng.tensor_reduce(idm[:, k:k + 1], tmp8[:, 0:W8], axis=AX, op=OP.min))
                            ch_v(eng.tensor_scalar(idm[:], idm[:], BIGI, None, op0=OP.add))
                            ch_v(eng.tensor_copy(idxi[:], idm[:]))
                            ch_v.signal(eng, c_f8, 1)

                @block3.scalar
                def _(eng):
                    for t in range(TO):
                        eng.wait_ge(c_f8, 2 * t + 1)
                        if t >= 1:
                            eng.wait_ge(c_f8, 2 * t)
                        ch_s(eng.activation(e8[:], f8[:], AF.Exp, bias=negm[:, 0:1]))
                        ch_s.signal(eng, c_ex, 1)
                        eng.wait_ge(c_f8, 2 * t + 2)
                        eng.wait_ge(c_gt, 128 * (t + 1))
                        for j in range(8):
                            if t >= 1:
                                eng.wait_ge(c_wa, 8 * (t - 1) + j + 1)
                            ch_s(eng.activation(wtm[j][:], vsel[t % 2][:, D * j:D * j + D],
                                                AF.Identity, bias=kc(KZ), scale=p8[t % 2][:, j:j + 1]))
                            ch_s.signal(eng, c_wm, 1)

                @block3.gpsimd
                def _(eng):
                    def wsum(tw):
                        if tw >= 2:
                            eng.wait_ge(c_gw[tw % 2], 16 * (tw // 2))
                        for j in range(8):
                            eng.wait_ge(c_wm, 8 * tw + j + 1)
                            if j == 0:
                                ch_g(eng.tensor_copy(gacc[tw % 2][:], wtm[0][:]))
                            elif j < 7:
                                ch_g(eng.tensor_tensor(gacc[tw % 2][:], gacc[tw % 2][:],
                                                       wtm[j][:], op=OP.add))
                            else:
                                ch_g(eng.tensor_tensor(gbf[tw % 2][:], gacc[tw % 2][:],
                                                       wtm[7][:], op=OP.add))
                            ch_g.signal(eng, c_wa, 1)
                        ch_g.signal(eng, c_gg, 1)

                    for t in range(TO):
                        if t >= 2:
                            wsum(t - 2)
                        eng.wait_ge(c_f8, 2 * t + 2)
                        if t >= 2:
                            eng.wait_ge(c_wm, 8 * (t - 1))
                        for j in range(8):
                            ins = eng.indirect_dma_start(
                                out=vsel[t % 2][:, D * j:D * j + D], out_offset=None,
                                in_=v_scr[:, :], in_offset=IndirectOffsetOnAxis(idxi[:, j:j + 1], 0))
                            ins.then_inc(c_gt, 16)
                    if TO >= 2:
                        wsum(TO - 2)
                    wsum(TO - 1)

                barrier(block3, sync_waits=[(c_gw[0], 16 * (TO // 2)), (c_gw[1], 16 * (TO // 2))])

        def phase3():
            C4 = ExitStack()
            with C4:
                _lsems = []
                def sem(n, _st=C4, _ls=_lsems):
                    h = _st.enter_context(nc.semaphore(n))
                    _ls.append(h)
                    return h
                sb4 = lambda n, sh, dt: C4.enter_context(nc.sbuf_tensor(n, list(sh), dt))
                w_o2 = sb4("w_o24", (128, SD * D), BF16)
                gf = [sb4("gf%d" % i, (128, SD * 512), BF16) for i in range(2)]
                pb = [sb4("pb%d" % i, (128, SD * 512), BF16) for i in range(2)]
                res = [sb4("res%d" % i, (128, SD * 512), BF16) for i in range(2)]
                rt = [sb4("rt%d" % i, (128, 4 * D), BF16) for i in range(2)]
                xo4 = [sb4("xo4%d" % i, (128, 4 * D), F32) for i in range(2)]
                ob = [sb4("ob%d" % i, (128, 4 * D), F32) for i in range(2)]
                ps4 = [C4.enter_context(nc.psum_tensor("p4ps%d" % i, [128, 512], F32)) for i in range(4)]

                d_wl = sem("d_wl")
                d_tx = [sem("d_tx0"), sem("d_tx1")]
                d_pl = [sem("d_pl0"), sem("d_pl1")]
                d_xl = [sem("d_xl0"), sem("d_xl1")]
                d_pe = sem("d_pe")
                d_evo = sem("d_evo")
                d_res = sem("d_res")
                d_ad = sem("d_ad")
                d_rw = [sem("d_rw0"), sem("d_rw1")]
                d_rt = [sem("d_rt0"), sem("d_rt1")]
                d_ow = [sem("d_ow0"), sem("d_ow1")]

                block4 = C4.enter_context(nc.Block())

                @block4.sync
                def _(eng):
                    eng.dma_start(out=w_o2[:], in_=wpre[IO2, :, :]).then_inc(d_wl, 16)
                    def out_write(q):
                        eng.wait_ge(d_ad, q + 1)
                        eng.dma_start(out=out_d[512 * q:512 * q + 512, :].rearrange("(t p) d -> p t d", p=128),
                                      in_=ob[q % 2][:].rearrange("p (t d) -> p t d", d=D)) \
                           .then_inc(d_ow[q % 2], 16)
                    for g in range(NG):
                        s = g % 2
                        if g >= 2:
                            eng.wait_ge(d_res, g - 1)
                        eng.dma_start(out=pb[s][:], in_=part_scr[g, :, :]).then_inc(d_pl[s], 16)
                        if g >= 2:
                            eng.wait_ge(d_ad, g - 1)
                        eng.dma_start(out=xo4[s][:].rearrange("p (t d) -> p t d", d=D),
                                      in_=xo[512 * g:512 * g + 512, :].rearrange("(t p) d -> p t d", p=128)) \
                           .then_inc(d_xl[s], 16)
                        eng.wait_ge(d_res, g + 1)
                        for o in range(SD):
                            eng.dma_start(out=res_scr[g, 128 * o:128 * o + 128, :],
                                          in_=res[s][:, 512 * o:512 * o + 512]).then_inc(d_rw[s], 16)
                        if g >= 1:
                            out_write(g - 1)
                    out_write(NG - 1)
                    eng.wait_ge(d_ow[0], 16 * ((NG + 1) // 2))
                    if NG >= 2:
                        eng.wait_ge(d_ow[1], 16 * (NG // 2))

                @block4.scalar
                def _(eng):
                    def rt_tx(q):
                        if q >= 2:
                            eng.wait_ge(d_ad, q - 1)
                        eng.wait_ge(d_rw[q % 2], 16 * SD * (q // 2 + 1))
                        for t in range(4):
                            eng.dma_start_transpose(out=rt[q % 2][:, D * t:D * t + D],
                                                    in_=res_scr[q, :, 128 * t:128 * t + 128]) \
                               .then_inc(d_rt[q % 2], 16)
                    for g in range(NG):
                        s = g % 2
                        if g >= 2:
                            eng.wait_ge(d_res, g - 1)
                        for i in range(SD):
                            eng.dma_start_transpose(out=gf[s][:, 512 * i:512 * i + 512],
                                                    in_=glob_scr[512 * g:512 * g + 512, 128 * i:128 * i + 128]) \
                               .then_inc(d_tx[s], 16)
                        if g >= 1:
                            rt_tx(g - 1)
                    rt_tx(NG - 1)

                @block4.vector
                def _(eng):
                    def xadd(q):
                        if q >= 2:
                            eng.wait_ge(d_ow[q % 2], 16 * (q // 2))
                        eng.wait_ge(d_rt[q % 2], 16 * 4 * (q // 2 + 1))
                        eng.wait_ge(d_xl[q % 2], 16 * (q // 2 + 1))
                        for t in range(4):
                            ch_v(eng.tensor_tensor(ob[q % 2][:, D * t:D * t + D], rt[q % 2][:, D * t:D * t + D],
                                                   xo4[q % 2][:, D * t:D * t + D], op=OP.add))
                        ch_v.signal(eng, d_ad, 1)
                    for g in range(NG):
                        s = g % 2
                        if g >= 2:
                            eng.wait_ge(d_rw[s], 16 * SD * (g // 2))
                        for o in range(SD):
                            gg = SD * g + o
                            eng.wait_ge(d_pe, gg + 1)
                            if o == 0:
                                eng.wait_ge(d_pl[s], 16 * (g // 2 + 1))
                            ch_v(eng.tensor_tensor(res[s][:, 512 * o:512 * o + 512], ps4[gg % 4][:, :],
                                                   pb[s][:, 512 * o:512 * o + 512], op=OP.add))
                            ch_v.signal(eng, d_evo, 1)
                        ch_v.signal(eng, d_res, 1)
                        if g >= 1:
                            xadd(g - 1)
                    xadd(NG - 1)

                @block4.tensor
                def _(eng):
                    eng.wait_ge(d_wl, 16)
                    for g in range(NG):
                        s = g % 2
                        eng.wait_ge(d_tx[s], 16 * SD * (g // 2 + 1))
                        for o in range(SD):
                            gg = SD * g + o
                            if gg >= 4:
                                eng.wait_ge(d_evo, gg - 3)
                            for i in range(SD):
                                ins = eng.matmul(ps4[gg % 4][:, :],
                                                 w_o2[:, D * i + 128 * o:D * i + 128 * o + 128],
                                                 gf[s][:, 512 * i:512 * i + 512],
                                                 start=(i == 0), stop=(i == SD - 1))
                                if i == SD - 1:
                                    ins.then_inc(d_pe, 1)

                barrier(block4, sync_waits=[(d_ow[0], 16 * ((NG + 1) // 2))]
                        + ([(d_ow[1], 16 * (NG // 2))] if NG >= 2 else []))

        phase0()
        phase1a()
        phase1b()
        phase2()
        phase3()
    return nc


_CACHE = {}


def _get_prog(cfg_key):
    if cfg_key not in _CACHE:
        _CACHE[cfg_key] = build_program(CFG(*cfg_key))
    return _CACHE[cfg_key]


def kernel(L0_attn_mu, L0_attn_Wq, L0_attn_bq, L0_attn_Wk, L0_attn_bk,
           L0_attn_Wv, L0_attn_bv, L0_attn_Wm1, L0_attn_bm1, L0_attn_Wm2,
           L0_attn_bm2, L0_attn_Wo, L0_attn_bo, L0_ffn_weight,
           L0_ffn_weight_scale, L0_ffn_bias):
    import ml_dtypes

    x = np.ascontiguousarray(np.asarray(L0_attn_mu, dtype=np.float32))
    B, N, D = x.shape
    c = CFG(D=D, N=N)
    nc = _get_prog((D, N))

    f32 = lambda a: np.ascontiguousarray(np.asarray(a, dtype=np.float32))
    w8 = np.asarray(L0_ffn_weight)
    w8b = np.ascontiguousarray(w8.view(np.uint8))
    ws = np.full((128, 1), np.float32(np.asarray(L0_ffn_weight_scale)), dtype=np.float32)

    shared = dict(
        wq=f32(L0_attn_Wq), wk=f32(L0_attn_Wk), wv=f32(L0_attn_Wv),
        bq=f32(L0_attn_bq), bk=f32(L0_attn_bk), bv=f32(L0_attn_bv),
        wm1=f32(L0_attn_Wm1), bm1=f32(L0_attn_bm1), wm2=f32(L0_attn_Wm2),
        bm2=f32(L0_attn_bm2), wo=f32(L0_attn_Wo), bo=f32(L0_attn_bo),
        fw8=w8b, fws=ws, fb=f32(L0_ffn_bias),
    )

    in_maps = []
    for core in range(8):
        b, r = core // 4, core % 4
        xb_c = x[b]
        tbs = [4 * j + r for j in range(c.TO)]
        xo_c = np.concatenate([xb_c[128 * tb:128 * tb + 128] for tb in tbs], 0)
        halo = np.zeros((128, D), dtype=np.float32)
        for j, tb in enumerate(tbs):
            if tb > 0:
                halo[4 * j:4 * j + 4] = xb_c[128 * tb - 4:128 * tb]
        dm = np.zeros((128, 512), dtype=np.float32)
        cols = np.arange(512)[None, :]
        rows = np.arange(128)[:, None]
        dm[cols > 128 * r + rows] = NEG
        m = dict(shared)
        m.update(xb=np.ascontiguousarray(xb_c), xo=np.ascontiguousarray(xo_c),
                 xh=halo, dmask=dm)
        in_maps.append(m)

    res = run_bass_kernel_spmd(nc, in_maps, core_ids=list(range(8)))
    out = np.empty((B, N, D), dtype=np.float32)
    for core in range(8):
        b, r = core // 4, core % 4
        oc = np.asarray(res.results[core]["out"])
        for j in range(c.TO):
            tb = 4 * j + r
            out[b, 128 * tb:128 * tb + 128] = oc[128 * j:128 * j + 128]
    return out


# version 26 (final)
# speedup vs baseline: 1.0056x; 1.0056x over previous
import numpy as np
from contextlib import ExitStack

import concourse.bass as bass
import concourse.mybir as mybir
from concourse.alu_op_type import AluOpType as OP
from concourse.bass_core_rust import IndirectOffsetOnAxis
from concourse.bass_utils import run_bass_kernel_spmd

AF = mybir.ActivationFunctionType
AX = mybir.AxisListType.X
F32 = mybir.dt.float32
BF16 = mybir.dt.bfloat16
I32 = mybir.dt.int32
U32 = mybir.dt.uint32
U8 = mybir.dt.uint8
FP8 = mybir.dt.float8e4

NEG = -1.0e30
BIGI = 65536.0


class CFG:
    def __init__(self, D=1024, N=16384):
        self.D = D
        self.SD = D // 128
        self.N = N
        self.T = N // 128
        self.TO = self.T // 4
        self.NP = N // 512
        self.NG = self.TO // 4
        assert self.TO % 4 == 0 and self.SD >= 2 and self.NP >= 2 and self.NG >= 1


class Chain:

    def __init__(self, nc, ctx, name):
        self.s = ctx.enter_context(nc.semaphore(name))
        self.n = 0

    def __call__(self, instr, w=True):
        if w and self.n > 0:
            instr._wait_ge(self.s, self.n)
        instr.then_inc(self.s, 1)
        self.n += 1
        return self.n

    def signal(self, eng, sem, k=1):
        i = eng.sem_inc(sem, k)
        if self.n > 0:
            i._wait_ge(self.s, self.n)
        return i


def build_program(c: CFG):
    nc = bass.Bass("TRN2", target_bir_lowering=False, debug=False)
    D, N, SD, TO, NP, NG = c.D, c.N, c.SD, c.TO, c.NP, c.NG

    dr = lambda name, shape, dt, kind="Internal": nc.dram_tensor(name, list(shape), dt, kind=kind)
    xb = dr("xb", (N, D), F32, "ExternalInput")
    xo = dr("xo", (TO * 128, D), F32, "ExternalInput")
    xh = dr("xh", (128, D), F32, "ExternalInput")
    wq_d = dr("wq", (D, D), F32, "ExternalInput")
    wk_d = dr("wk", (D, D), F32, "ExternalInput")
    wv_d = dr("wv", (D, D), F32, "ExternalInput")
    bq_d = dr("bq", (D,), F32, "ExternalInput")
    bk_d = dr("bk", (D,), F32, "ExternalInput")
    bv_d = dr("bv", (D,), F32, "ExternalInput")
    wm1_d = dr("wm1", (2 * D, D), F32, "ExternalInput")
    bm1_d = dr("bm1", (D,), F32, "ExternalInput")
    wm2_d = dr("wm2", (D, D), F32, "ExternalInput")
    bm2_d = dr("bm2", (D,), F32, "ExternalInput")
    wo_d = dr("wo", (2 * D, D), F32, "ExternalInput")
    bo_d = dr("bo", (D,), F32, "ExternalInput")
    fw8_d = dr("fw8", (D, D), U8, "ExternalInput")
    fws_d = dr("fws", (128, 1), F32, "ExternalInput")
    fb_d = dr("fb", (D,), F32, "ExternalInput")
    dmask_d = dr("dmask", (128, 512), F32, "ExternalInput")
    out_d = dr("out", (TO * 128, D), F32, "ExternalOutput")
    xn_scr = dr("xn_scr", (N, D), BF16)
    k_scr = dr("k_scr", (NP, 128, SD * 512), BF16)
    vfm_scr = dr("vfm_scr", (NP, D, 512), BF16)
    v_scr = dr("v_scr", (N, D), BF16)
    xno_scr = dr("xno_scr", (TO * 128, D), BF16)
    a_scr = dr("a_scr", (TO * 128, D), BF16)
    xh_scr = dr("xh_scr", (128, D), BF16)
    q_scr = dr("q_scr", (NG, 128, SD * 512), BF16)
    w2t_scr = dr("w2t_scr", (D, D), BF16)
    bmt_scr = dr("bmt_scr", (D, D), BF16)
    part_scr = dr("part_scr", (NG, 128, SD * 512), BF16)
    glob_scr = dr("glob_scr", (TO * 128, D), BF16)
    res_scr = dr("res_scr", (NG, D, 512), BF16)
    wpre = dr("wpre", (8, 128, SD * D), BF16)
    IWQ, IWK, IWV, IA1, IB1, IO2, IWC, IBM = range(8)

    QSC = float(1.0 / np.sqrt(D))
    PS = ExitStack()
    with PS:
        sbp = lambda n, sh, dt: PS.enter_context(nc.sbuf_tensor(n, list(sh), dt))
        sem = lambda n: PS.enter_context(nc.semaphore(n))

        bias = sbp("bias", (128, 64), F32)
        BQ, BK, BV, BM1, BOB, BO, FB, BM2 = range(8)
        bcv = lambda slot, o: bias[:, 8 * slot + o:8 * slot + o + 1]
        dmask = sbp("dmask_sb", (128, 512), F32)
        fws_sb = sbp("fws_sb", (128, 1), F32)
        whalo = sbp("whalo", (128, SD * 128), F32)
        kcn = sbp("kcn", (128, 16), F32)
        KEPS, KQRT, K32, K12, KZ = 0, 1, 2, 3, 15
        kc = lambda i: kcn[:, i:i + 1]

        ch_v = Chain(nc, PS, "ch_v")
        ch_s = Chain(nc, PS, "ch_s")
        ch_g = Chain(nc, PS, "ch_g")
        s_bar = sem("s_bar")
        bar = [0]

        def barrier(block, sync_waits=(), pe_wait=None, clear=()):
            base = bar[0]
            bar[0] += 6 if clear else 5

            @block.sync
            def _(eng):
                for (s, v) in sync_waits:
                    eng.wait_ge(s, v)
                eng.sem_inc(s_bar, 1)
                eng.wait_ge(s_bar, base + 5)
                if clear:
                    for h in clear:
                        eng.sem_clear(h)
                    eng.sem_inc(s_bar, 1)

            @block.scalar
            def _(eng):
                ch_s.signal(eng, s_bar, 1)
                eng.wait_ge(s_bar, base + (6 if clear else 5))

            @block.vector
            def _(eng):
                ch_v.signal(eng, s_bar, 1)
                eng.wait_ge(s_bar, base + (6 if clear else 5))

            @block.gpsimd
            def _(eng):
                ch_g.signal(eng, s_bar, 1)
                eng.wait_ge(s_bar, base + (6 if clear else 5))

            @block.tensor
            def _(eng):
                i = eng.sem_inc(s_bar, 1)
                if pe_wait is not None:
                    i._wait_ge(pe_wait[0], pe_wait[1])
                eng.wait_ge(s_bar, base + (6 if clear else 5))

        def phase0():
            C0 = ExitStack()
            with C0:
                _lsems = []
                def sem(n, _st=C0, _ls=_lsems):
                    h = _st.enter_context(nc.semaphore(n))
                    _ls.append(h)
                    return h
                sb0 = lambda n, sh, dt: C0.enter_context(nc.sbuf_tensor(n, list(sh), dt))
                stg = [sb0("p0stg%d" % i, (128, D), F32) for i in range(6)]
                bcast = [sb0("p0bc%d" % i, (128, D), BF16) for i in range(2)]
                w_o1 = sb0("p0wo1", (128, SD * D), BF16)
                w_b1 = sb0("p0wb1", (128, SD * D), BF16)
                wm2tT = sb0("p0wm2tT", (128, SD * D), BF16)
                wbmT = sb0("p0wbmT", (128, SD * D), BF16)
                w_c0 = sb0("p0wc0", (128, SD * D), BF16)
                u8s = sb0("p0u8", (128, D), U8)
                mk8 = sb0("p0mk8", (128, D), U8)
                b28 = sb0("p0b28", (128, D), U8)
                b38 = sb0("p0b38", (128, D), U8)
                f32a = sb0("p0f32a", (128, D), F32)
                f32b = sb0("p0f32b", (128, D), F32)
                f32c = sb0("p0f32c", (128, D), F32)
                i32a = sb0("p0i32a", (128, D), I32)
                dect = sb0("p0dect", (128, D), BF16)
                xhx = sb0("p0xhx", (128, D), F32)
                xhsq = sb0("p0xhsq", (128, D), F32)
                xhb = sb0("p0xhb", (128, D), BF16)
                xhf = sb0("p0xhf", (128, SD * 128), BF16)
                bm2b = sb0("p0bm2b", (128, 8), BF16)
                ps0 = [C0.enter_context(nc.psum_tensor("p0ps%d" % i, [128, 512], F32)) for i in range(2)]
                psd = C0.enter_context(nc.psum_tensor("p0psd", [128, 8], F32))

                z_ld = [sem("z_ld%d" % i) for i in range(6)]
                z_u8 = sem("z_u8")
                z_sm = sem("z_sm")
                z_wp = [sem("z_wp0"), sem("z_wp1")]
                z_wb = sem("z_wb")
                z_wb1 = sem("z_wb1")
                z_wxh = sem("z_wxh")
                z_wcm = sem("z_wcm")
                z_tm2 = sem("z_tm2")
                z_tbm = sem("z_tbm")
                z_txh = sem("z_txh")
                z_v = sem("z_v")
                z_s = sem("z_s")
                z_pe = sem("z_pe")
                z_ev = sem("z_ev")

                casts = []
                for wi, (src, r0) in [(IWQ, (wq_d, 0)), (IWK, (wk_d, 0)), (IWV, (wv_d, 0)),
                                      (IA1, (wm1_d, 0)), (IO2, (wo_d, D))]:
                    for i in range(SD):
                        casts.append((src, r0 + 128 * i, "w", (wi, i)))
                for i in range(SD):
                    casts.append((wm1_d, D + 128 * i, "k", (w_b1, i)))
                for i in range(SD):
                    casts.append((wo_d, 128 * i, "k", (w_o1, i)))
                for i in range(SD):
                    casts.append((wm2_d, 128 * i, "s", i))
                NCAST = len(casts)
                slot_wr = {0: 0, 1: 0}
                wr_prior = []
                nws = {0: 0, 1: 0}
                for n, (src, r0, kind, arg) in enumerate(casts):
                    if kind in ("w", "s"):
                        wr_prior.append(nws[n % 2])
                        nws[n % 2] += 1
                    else:
                        wr_prior.append(None)
                NWR0, NWR1 = nws[0], nws[1]
                NH = max(1, D // 512)
                WH = min(D, 512)
                NGC = SD * NH
                E_CAST = NCAST
                E_DEC = E_CAST + SD
                E_HALOV = E_DEC + 1
                E_HALOB = E_HALOV + 1
                E_BM2B = E_HALOB + 1

                block0 = C0.enter_context(nc.Block())

                @block0.vector
                def _(eng):
                    for (slot, val) in [(KEPS, 1e-5), (KQRT, 0.25), (K32, QSC), (K12, 1e-12),
                                        (KZ, 0.0)]:
                        ch_v(eng.memset(kc(slot), val))
                    for n, (src, r0, kind, arg) in enumerate(casts):
                        eng.wait_ge(z_ld[n % 6], 16 * (n // 6 + 1))
                        if kind in ("w", "s"):
                            if wr_prior[n]:
                                eng.wait_ge(z_wp[n % 2], 16 * wr_prior[n])
                            ch_v(eng.tensor_copy(bcast[n % 2][:], stg[n % 6][:]))
                        else:
                            dst, i = arg
                            ch_v(eng.tensor_copy(dst[:, D * i:D * i + D], stg[n % 6][:]))
                        ch_v.signal(eng, z_v, 1)
                    for i in range(SD):
                        eng.wait_ge(z_u8, 16 * (i + 1))
                        if i >= 1:
                            eng.wait_ge(z_wb, 16 * i)
                        u = u8s[:]
                        ch_v(eng.tensor_scalar(mk8[:], u, 0x78, None, op0=OP.bitwise_and))
                        ch_v(eng.tensor_scalar(mk8[:], mk8[:], 0x78, None, op0=OP.is_equal))
                        ch_v(eng.tensor_scalar(b28[:], u, 0xF7, None, op0=OP.bitwise_and))
                        ch_v(eng.tensor_copy(b38[:], u))
                        ch_v(eng.copy_predicated(b38[:], mk8[:], b28[:]))
                        ch_v(eng.tensor_copy(f32c[:], b38[:].bitcast(FP8)))
                        ch_v(eng.tensor_copy(f32a[:], mk8[:]))
                        ch_v(eng.tensor_scalar(f32a[:], f32a[:], 1.0, None, op0=OP.add))
                        ch_v(eng.tensor_tensor(dect[:], f32c[:], f32a[:], op=OP.mult))
                        ch_v.signal(eng, z_v, 1)
                    eng.wait_ge(z_sm, 16 * 10)
                    ch_v(eng.reduce_sum(kcn[:, 4:5], xhx[:], axis=AX))
                    ch_v(eng.tensor_scalar(kcn[:, 4:5], kcn[:, 4:5], -1.0 / D, None, op0=OP.mult))
                    ch_v(eng.tensor_scalar(xhx[:], xhx[:], kcn[:, 4:5], None, op0=OP.add))
                    ch_v(eng.tensor_tensor(xhsq[:], xhx[:], xhx[:], op=OP.mult))
                    ch_v(eng.reduce_sum(kcn[:, 5:6], xhsq[:], axis=AX))
                    ch_v(eng.tensor_scalar(kcn[:, 5:6], kcn[:, 5:6], 1.0 / D, None, op0=OP.mult))
                    ch_v.signal(eng, z_v, 1)
                    eng.wait_ge(z_s, 1)
                    ch_v(eng.reciprocal(kcn[:, 6:7], kcn[:, 7:8]))
                    ch_v(eng.tensor_scalar(xhb[:], xhx[:], kcn[:, 6:7], None, op0=OP.mult))
                    ch_v.signal(eng, z_v, 1)
                    ch_v(eng.tensor_scalar(bias[:, 8 * BQ:8 * BQ + SD], bias[:, 8 * BQ:8 * BQ + SD],
                                           QSC, None, op0=OP.mult))
                    ch_v(eng.tensor_copy(bm2b[:, 0:SD], bias[:, 8 * BM2:8 * BM2 + SD]))
                    ch_v.signal(eng, z_v, 1)
                    for g in range(NGC):
                        isub, jh = divmod(g, NH)
                        eng.wait_ge(z_pe, g + 1)
                        ch_v(eng.tensor_scalar(w_c0[:, D * isub + WH * jh:D * isub + WH * jh + WH],
                                               ps0[g % 2][:, 0:WH], kc(KQRT), None, op0=OP.mult))
                        ch_v.signal(eng, z_ev, 1)
                    for g in range(SD):
                        eng.wait_ge(z_pe, NGC + g + 1)
                        ch_v(eng.tensor_copy(whalo[:, 128 * g:128 * g + 128], ps0[g % 2][:, 0:128]))
                        ch_v.signal(eng, z_ev, 1)
                    eng.wait_ge(z_pe, NGC + SD + 1)
                    ch_v(eng.tensor_tensor(bias[:, 8 * BOB:8 * BOB + SD], psd[:, 0:SD],
                                           bias[:, 8 * BO:8 * BO + SD], op=OP.add))
                    ch_v(eng.tensor_tensor(bias[:, 8 * BOB:8 * BOB + SD], bias[:, 8 * BOB:8 * BOB + SD],
                                           bias[:, 8 * FB:8 * FB + SD], op=OP.add))
                    ch_v.signal(eng, z_ev, 1)

                @block0.sync
                def _(eng):
                    def bcast_write(n):
                        (src, r0, kind, arg) = casts[n]
                        if kind == "w":
                            wi, i = arg
                            eng.wait_ge(z_v, n + 1)
                            eng.dma_start(out=wpre[wi, :, D * i:D * i + D],
                                          in_=bcast[n % 2][:]).then_inc(z_wp[n % 2], 16)
                        elif kind == "s":
                            i = arg
                            eng.wait_ge(z_v, n + 1)
                            eng.dma_start(out=w2t_scr[128 * i:128 * i + 128, :],
                                          in_=bcast[n % 2][:]).then_inc(z_wp[n % 2], 16)

                    for n, (src, r0, kind, arg) in enumerate(casts):
                        if n >= 6:
                            eng.wait_ge(z_v, n - 5)
                        eng.dma_start(out=stg[n % 6][:], in_=src[r0:r0 + 128, :]).then_inc(z_ld[n % 6], 16)
                        if n >= 2:
                            bcast_write(n - 2)
                    bcast_write(NCAST - 2)
                    bcast_write(NCAST - 1)
                    for i in range(SD):
                        if i >= 1:
                            eng.wait_ge(z_v, E_CAST + i)
                        eng.dma_start(out=u8s[:], in_=fw8_d[128 * i:128 * i + 128, :]).then_inc(z_u8, 16)
                        if i >= 1:
                            eng.wait_ge(z_v, E_CAST + i)
                            eng.dma_start(out=bmt_scr[128 * (i - 1):128 * (i - 1) + 128, :],
                                          in_=dect[:]).then_inc(z_wb, 16)
                    eng.wait_ge(z_v, E_CAST + SD)
                    eng.dma_start(out=bmt_scr[128 * (SD - 1):128 * SD, :], in_=dect[:]).then_inc(z_wb, 16)
                    for (slot, src) in [(BQ, bq_d), (BK, bk_d), (BV, bv_d), (BM1, bm1_d),
                                        (BO, bo_d), (FB, fb_d), (BM2, bm2_d)]:
                        eng.dma_start(out=bias[:, 8 * slot:8 * slot + SD],
                                      in_=src[:].rearrange("(j p) -> p j", p=128),
                                      allow_slow_non_contiguous=True).then_inc(z_sm, 16)
                    eng.dma_start(out=dmask[:], in_=dmask_d[:, :]).then_inc(z_sm, 16)
                    eng.dma_start(out=fws_sb[:, 0:1], in_=fws_d[:, 0:1]).then_inc(z_sm, 16)
                    eng.dma_start(out=xhx[:], in_=xh[:, :]).then_inc(z_sm, 16)
                    eng.wait_ge(z_v, E_CAST)
                    eng.dma_start(out=wpre[IB1, :, :], in_=w_b1[:, :]).then_inc(z_wb1, 16)
                    eng.wait_ge(z_v, E_HALOB)
                    eng.dma_start(out=xh_scr[:, :], in_=xhb[:]).then_inc(z_wxh, 16)
                    eng.wait_ge(z_ev, NGC)
                    eng.dma_start(out=wpre[IWC, :, :], in_=w_c0[:, :]).then_inc(z_wcm, 16)
                    eng.wait_ge(z_tbm, 16 * SD)
                    eng.dma_start(out=wpre[IBM, :, :], in_=wbmT[:, :]).then_inc(z_wcm, 16)

                @block0.scalar
                def _(eng):
                    eng.wait_ge(z_v, E_HALOV)
                    ch_s(eng.activation(kcn[:, 7:8], kcn[:, 5:6], AF.Sqrt, bias=kc(KEPS)))
                    ch_s.signal(eng, z_s, 1)
                    eng.wait_ge(z_wp[0], 16 * NWR0)
                    eng.wait_ge(z_wp[1], 16 * NWR1)
                    for m in range(SD):
                        eng.dma_start_transpose(out=wm2tT[:, D * m:D * m + D],
                                                in_=w2t_scr[:, 128 * m:128 * m + 128]).then_inc(z_tm2, 16)
                    eng.wait_ge(z_wb, 16 * SD)
                    for k in range(SD):
                        eng.dma_start_transpose(out=wbmT[:, D * k:D * k + D],
                                                in_=bmt_scr[:, 128 * k:128 * k + 128]).then_inc(z_tbm, 16)
                    eng.wait_ge(z_wxh, 16)
                    for i in range(SD):
                        eng.dma_start_transpose(out=xhf[:, 128 * i:128 * i + 128],
                                                in_=xh_scr[:, 128 * i:128 * i + 128]).then_inc(z_txh, 16)

                @block0.tensor
                def _(eng):
                    eng.wait_ge(z_v, E_CAST)
                    eng.wait_ge(z_tm2, 16 * SD)
                    g = 0
                    for isub in range(SD):
                        for jh in range(NH):
                            if g >= 2:
                                eng.wait_ge(z_ev, g - 1)
                            for m in range(SD):
                                ins = eng.matmul(ps0[g % 2][:, 0:WH],
                                                 wm2tT[:, D * m + 128 * isub:D * m + 128 * isub + 128],
                                                 w_o1[:, D * m + WH * jh:D * m + WH * jh + WH],
                                                 start=(m == 0), stop=(m == SD - 1))
                                if m == SD - 1:
                                    ins.then_inc(z_pe, 1)
                            g += 1
                    eng.wait_ge(z_txh, 16 * SD)
                    for o in range(SD):
                        gg = NGC + o
                        eng.wait_ge(z_ev, gg - 1)
                        for i in range(SD):
                            ins = eng.matmul(ps0[gg % 2][:, 0:128],
                                             w_b1[:, D * i + 128 * o:D * i + 128 * o + 128],
                                             xhf[:, 128 * i:128 * i + 128],
                                             start=(i == 0), stop=(i == SD - 1))
                            if i == SD - 1:
                                ins.then_inc(z_pe, 1)
                    eng.wait_ge(z_v, E_BM2B)
                    eng.wait_ge(z_ev, NGC + SD)
                    for jsub in range(SD):
                        for i in range(SD):
                            ins = eng.matmul(psd[:, jsub:jsub + 1],
                                             w_o1[:, D * i + 128 * jsub:D * i + 128 * jsub + 128],
                                             bm2b[:, i:i + 1],
                                             start=(i == 0), stop=(i == SD - 1))
                            if i == SD - 1 and jsub == SD - 1:
                                ins.then_inc(z_pe, 1)

                barrier(block0,
                        sync_waits=[(z_wb1, 16), (z_wcm, 32), (z_ev, NGC + SD + 1)])

        def phase1a():
            C1 = ExitStack()
            with C1:
                _lsems = []
                def sem(n, _st=C1, _ls=_lsems):
                    h = _st.enter_context(nc.semaphore(n))
                    _ls.append(h)
                    return h
                sb1 = lambda n, sh, dt: C1.enter_context(nc.sbuf_tensor(n, list(sh), dt))
                w_k = sb1("w_k1", (128, SD * D), BF16)
                w_v = sb1("w_v1", (128, SD * D), BF16)
                xbuf = [sb1("xbuf%d" % i, (128, 4 * D), F32) for i in range(2)]
                xc32 = sb1("xc32", (128, 4 * D), F32)
                sq32 = sb1("sq32", (128, D), F32)
                st4 = [sb1("st4_%d" % i, (128, 16), F32) for i in range(2)]
                xnb = [sb1("xnb%d" % i, (128, 4 * D), BF16) for i in range(2)]
                xf = [sb1("xf%d" % i, (128, SD * 512), BF16) for i in range(2)]
                ksb = [sb1("ksb%d" % i, (128, SD * 512), BF16) for i in range(2)]
                vsb = [sb1("vsb%d" % i, (128, SD * 512), BF16) for i in range(2)]
                vtb = [sb1("vtb%d" % i, (128, 4 * D), BF16) for i in range(2)]
                xob = [sb1("xob%d" % i, (128, D), F32) for i in range(2)]
                xoc = sb1("xoc", (128, D), F32)
                ost = [sb1("ost%d" % i, (128, 8), F32) for i in range(2)]
                xnob = [sb1("xnob%d" % i, (128, D), BF16) for i in range(2)]
                yq = sb1("yq", (128, D), F32)
                q8t = sb1("q8t", (128, D), FP8)
                abuf = sb1("abuf", (128, D), BF16)
                aout = [sb1("aout%d" % i, (128, D), BF16) for i in range(2)]
                ps1 = [C1.enter_context(nc.psum_tensor("p1ps%d" % i, [128, 512], F32)) for i in range(6)]
                HB = SD // 2

                a_wl = sem("a_wl")
                a_xld = [sem("a_xld0"), sem("a_xld1")]
                a_old = [sem("a_old0"), sem("a_old1")]
                a_xnw = [sem("a_xnw0"), sem("a_xnw1")]
                a_txf = [sem("a_txf0"), sem("a_txf1")]
                a_txv = [sem("a_txv0"), sem("a_txv1")]
                a_var = sem("a_var")
                a_sq = sem("a_sq")
                a_xnr = sem("a_xnr")
                a_pe = sem("a_pe")
                a_ev = sem("a_ev")
                a_kw = [sem("a_kw0"), sem("a_kw1")]
                a_vfw = [sem("a_vfw0"), sem("a_vfw1")]
                a_vtw = [sem("a_vtw0"), sem("a_vtw1")]
                a_onw = [sem("a_onw0"), sem("a_onw1")]
                a_aw = [sem("a_aw0"), sem("a_aw1")]
                a_vch = sem("a_vch")
                a_och = sem("a_och")

                block1 = C1.enter_context(nc.Block())

                @block1.sync
                def _(eng):
                    eng.dma_start(out=w_k[:], in_=wpre[IWK, :, :]).then_inc(a_wl, 16)
                    eng.dma_start(out=w_v[:], in_=wpre[IWV, :, :]).then_inc(a_wl, 16)

                    def load_x(q):
                        eng.dma_start(out=xbuf[q % 2][:].rearrange("p (t d) -> p t d", d=D),
                                      in_=xb[512 * q:512 * q + 512, :].rearrange("(t p) d -> p t d", p=128)) \
                           .then_inc(a_xld[q % 2], 16)

                    def load_xo(q):
                        eng.dma_start(out=xob[q % 2][:], in_=xo[128 * q:128 * q + 128, :]) \
                           .then_inc(a_old[q % 2], 16)

                    def kv_writes(q):
                        sq = q % 2
                        eng.wait_ge(a_ev, 2 * SD * q + SD)
                        eng.dma_start(out=k_scr[q, :, :], in_=ksb[sq][:]).then_inc(a_kw[sq], 16)
                        eng.wait_ge(a_ev, 2 * SD * (q + 1))
                        for o in range(SD):
                            eng.dma_start(out=vfm_scr[q, 128 * o:128 * o + 128, :],
                                          in_=vsb[sq][:, 512 * o:512 * o + 512]).then_inc(a_vfw[sq], 16)

                    def vtb_chain(q):
                        sq = q % 2
                        if q >= 2:
                            eng.wait_ge(a_vtw[sq], 16 * (q // 2))
                        eng.wait_ge(a_vfw[sq], 16 * SD * (q // 2 + 1))
                        for t in range(4):
                            eng.dma_start_transpose(out=vtb[sq][:, D * t:D * t + D],
                                                    in_=vfm_scr[q, :, 128 * t:128 * t + 128]) \
                               .then_inc(a_txv[sq], 16)
                        eng.wait_ge(a_txv[sq], 16 * 4 * (q // 2 + 1))
                        eng.dma_start(out=v_scr[512 * q:512 * q + 512, :].rearrange("(t p) d -> p t d", p=128),
                                      in_=vtb[sq][:].rearrange("p (t d) -> p t d", d=D)) \
                           .then_inc(a_vtw[sq], 16)

                    for q in range(min(2, NP)):
                        load_x(q)
                        load_xo(q)
                    for p in range(NP):
                        s = p % 2
                        if p + 2 < NP:
                            eng.wait_ge(a_vch, p + 1)
                            load_x(p + 2)
                            eng.wait_ge(a_och, p + 1)
                            load_xo(p + 2)
                        eng.wait_ge(a_xnr, 3 * p + 1)
                        eng.dma_start(out=xn_scr[512 * p:512 * p + 512, :].rearrange("(t p) d -> p t d", p=128),
                                      in_=xnb[s][:].rearrange("p (t d) -> p t d", d=D)) \
                           .then_inc(a_xnw[s], 16)
                        eng.wait_ge(a_xnr, 3 * p + 2)
                        eng.dma_start(out=xno_scr[128 * p:128 * p + 128, :], in_=xnob[s][:]).then_inc(a_onw[s], 16)
                        eng.wait_ge(a_xnr, 3 * p + 3)
                        eng.dma_start(out=a_scr[128 * p:128 * p + 128, :], in_=aout[s][:]).then_inc(a_aw[s], 16)
                        if p >= 1:
                            kv_writes(p - 1)
                        if p >= 2:
                            vtb_chain(p - 2)
                    kv_writes(NP - 1)
                    for q in (NP - 2, NP - 1):
                        if q >= 0:
                            vtb_chain(q)

                @block1.vector
                def _(eng):
                    for p in range(NP):
                        s = p % 2
                        eng.wait_ge(a_xld[s], 16 * (p // 2 + 1))
                        x4 = xbuf[s][:].rearrange("p (t d) -> p t d", d=D)
                        ch_v(eng.reduce_sum(st4[s][:, 0:4].rearrange("p (t f) -> p t f", f=1), x4, axis=AX))
                        ch_v(eng.tensor_scalar(st4[s][:, 0:4], st4[s][:, 0:4], -1.0 / D, None, op0=OP.mult))
                        for t in range(4):
                            ch_v(eng.tensor_scalar(xc32[:, D * t:D * t + D], xbuf[s][:, D * t:D * t + D],
                                                   st4[s][:, t:t + 1], None, op0=OP.add))
                        ch_v(eng.tensor_tensor(xbuf[s][:], xc32[:], xc32[:], op=OP.mult))
                        ch_v(eng.reduce_sum(st4[s][:, 4:8].rearrange("p (t f) -> p t f", f=1),
                                            xbuf[s][:].rearrange("p (t d) -> p t d", d=D), axis=AX))
                        ch_v.signal(eng, a_vch, 1)
                        ch_v(eng.tensor_scalar(st4[s][:, 4:8], st4[s][:, 4:8], 1.0 / D, None, op0=OP.mult))
                        ch_v.signal(eng, a_var, 1)
                        eng.wait_ge(a_sq, 2 * p + 1)
                        if p >= 2:
                            eng.wait_ge(a_xnw[s], 16 * (p // 2))
                        ch_v(eng.reciprocal(st4[s][:, 8:12], st4[s][:, 12:16]))
                        for t in range(4):
                            ch_v(eng.tensor_scalar(xnb[s][:, D * t:D * t + D], xc32[:, D * t:D * t + D],
                                                   st4[s][:, 8 + t:9 + t], None, op0=OP.mult))
                        ch_v.signal(eng, a_xnr, 1)
                        eng.wait_ge(a_old[s], 16 * (p // 2 + 1))
                        ch_v(eng.reduce_sum(ost[s][:, 0:1], xob[s][:], axis=AX))
                        ch_v(eng.tensor_scalar(ost[s][:, 0:1], ost[s][:, 0:1], -1.0 / D, None, op0=OP.mult))
                        ch_v(eng.tensor_scalar(xoc[:], xob[s][:], ost[s][:, 0:1], None, op0=OP.add))
                        ch_v.signal(eng, a_och, 1)
                        ch_v(eng.tensor_tensor(sq32[:], xoc[:], xoc[:], op=OP.mult))
                        ch_v(eng.reduce_sum(ost[s][:, 1:2], sq32[:], axis=AX))
                        ch_v(eng.tensor_scalar(ost[s][:, 1:2], ost[s][:, 1:2], 1.0 / D, None, op0=OP.mult))
                        ch_v.signal(eng, a_var, 1)
                        eng.wait_ge(a_sq, 2 * p + 2)
                        if p >= 2:
                            eng.wait_ge(a_onw[s], 16 * (p // 2))
                            eng.wait_ge(a_aw[s], 16 * (p // 2))
                        ch_v(eng.reciprocal(ost[s][:, 2:3], ost[s][:, 3:4]))
                        ch_v(eng.tensor_scalar(xoc[:], xoc[:], ost[s][:, 2:3], None, op0=OP.mult))
                        ch_v(eng.tensor_copy(xnob[s][:], xoc[:]))
                        ch_v.signal(eng, a_xnr, 1)
                        ch_v(eng.tensor_reduce(ost[s][:, 4:5], xoc[:], axis=AX, op=OP.max,
                                               apply_absolute_value=True))
                        ch_v(eng.tensor_scalar(ost[s][:, 4:5], ost[s][:, 4:5], kc(K12), None, op0=OP.max))
                        ch_v(eng.reciprocal(ost[s][:, 5:6], ost[s][:, 4:5]))
                        ch_v(eng.tensor_scalar(ost[s][:, 5:6], ost[s][:, 5:6], 448.0, None, op0=OP.mult))
                        ch_v(eng.tensor_scalar(yq[:], xoc[:], ost[s][:, 5:6], 0.5, op0=OP.mult, op1=OP.mult))
                        ch_v(eng.tensor_copy(q8t[:], yq[:]))
                        ch_v(eng.reciprocal(ost[s][:, 6:7], ost[s][:, 5:6]))
                        ch_v(eng.tensor_scalar(ost[s][:, 6:7], ost[s][:, 6:7], 2.0, fws_sb[:, 0:1],
                                               op0=OP.mult, op1=OP.mult))
                        ch_v(eng.tensor_scalar(abuf[:], q8t[:], ost[s][:, 6:7], None, op0=OP.mult))
                        ch_v(eng.tensor_copy(aout[s][:].bitcast(U8), abuf[:].bitcast(U8)))
                        ch_v.signal(eng, a_xnr, 1)

                @block1.scalar
                def _(eng):
                    def sqrt_main(q):
                        eng.wait_ge(a_var, 2 * q + 1)
                        ch_s(eng.activation(st4[q % 2][:, 12:16], st4[q % 2][:, 4:8], AF.Sqrt, bias=kc(KEPS)))
                        ch_s.signal(eng, a_sq, 1)

                    def sqrt_own(q):
                        eng.wait_ge(a_var, 2 * q + 2)
                        ch_s(eng.activation(ost[q % 2][:, 3:4], ost[q % 2][:, 1:2], AF.Sqrt, bias=kc(KEPS)))
                        ch_s.signal(eng, a_sq, 1)

                    def transposes(q):
                        sq = q % 2
                        if q >= 2:
                            eng.wait_ge(a_pe, 2 * SD * (q - 1))
                        eng.wait_ge(a_xnw[sq], 16 * (q // 2 + 1))
                        for i in range(SD):
                            eng.dma_start_transpose(out=xf[sq][:, 512 * i:512 * i + 512],
                                                    in_=xn_scr[512 * q:512 * q + 512, 128 * i:128 * i + 128]) \
                               .then_inc(a_txf[sq], 16)

                    POS_OWN = min(3, SD - 1)
                    POS_TR = min(5, SD - 1)
                    sqrt_main(0)
                    sqrt_own(0)
                    transposes(0)
                    for p in range(NP):
                        s = p % 2
                        if p >= 2:
                            eng.wait_ge(a_kw[s], 16 * (p // 2))
                        for o in range(SD):
                            G = 2 * SD * p + o
                            eng.wait_ge(a_pe, G + 1)
                            ch_s(eng.activation(ksb[s][:, 512 * o:512 * o + 512],
                                                ps1[G % 6][:, :],
                                                AF.Identity, bias=bcv(BK, o)))
                            ch_s.signal(eng, a_ev, 1)
                        if p + 1 < NP:
                            sqrt_main(p + 1)
                        if p >= 2:
                            eng.wait_ge(a_vfw[s], 16 * SD * (p // 2))
                        for o in range(SD):
                            G = 2 * SD * p + SD + o
                            eng.wait_ge(a_pe, G + 1)
                            ch_s(eng.activation(vsb[s][:, 512 * o:512 * o + 512],
                                                ps1[G % 6][:, :],
                                                AF.Identity, bias=bcv(BV, o)))
                            ch_s.signal(eng, a_ev, 1)
                            if o == POS_OWN and p + 1 < NP:
                                sqrt_own(p + 1)
                            if o == POS_TR and p + 1 < NP:
                                transposes(p + 1)

                @block1.tensor
                def _(eng):
                    eng.wait_ge(a_wl, 32)
                    for p in range(NP):
                        s = p % 2
                        eng.wait_ge(a_txf[s], 16 * SD * (p // 2 + 1))
                        for mat in range(2):
                            w = w_k if mat == 0 else w_v
                            for o in range(SD):
                                G = 2 * SD * p + SD * mat + o
                                if G >= 6:
                                    eng.wait_ge(a_ev, G - 5)
                                dst = ps1[G % 6][:, :]
                                for i in range(SD):
                                    ins = eng.matmul(dst,
                                                     w[:, D * i + 128 * o:D * i + 128 * o + 128],
                                                     xf[s][:, 512 * i:512 * i + 512],
                                                     start=(i == 0), stop=(i == SD - 1))
                                    if i == SD - 1:
                                        ins.then_inc(a_pe, 1)

                barrier(block1, sync_waits=[
                    (a_kw[0], 16 * (NP // 2)), (a_kw[1], 16 * (NP // 2)),
                    (a_vtw[0], 16 * (NP // 2)), (a_vtw[1], 16 * (NP // 2)),
                    (a_onw[0], 16 * (NP // 2)), (a_onw[1], 16 * (NP // 2)),
                    (a_aw[0], 16 * (NP // 2)), (a_aw[1], 16 * (NP // 2)),
                    (a_vfw[0], 16 * SD * (NP // 2)), (a_vfw[1], 16 * SD * (NP // 2))])

        def phase1b():
            C2 = ExitStack()
            with C2:
                _lsems = []
                def sem(n, _st=C2, _ls=_lsems):
                    h = _st.enter_context(nc.semaphore(n))
                    _ls.append(h)
                    return h
                sb2 = lambda n, sh, dt: C2.enter_context(nc.sbuf_tensor(n, list(sh), dt))
                w_q = sb2("w_q2", (128, SD * D), BF16)
                w_a1 = sb2("w_a12", (128, SD * D), BF16)
                w_b1 = sb2("w_b12", (128, SD * D), BF16)
                w_c = sb2("w_c2", (128, SD * D), BF16)
                w_bm = sb2("w_bm2", (128, SD * D), BF16)
                xof_ = sb2("xof", (128, SD * 512), BF16)
                af_ = sb2("af", (128, SD * 512), BF16)
                qsb_ = sb2("qsb", (128, SD * 512), BF16)
                ub = sb2("ub", (128, SD * 512), BF16)
                wcat = sb2("wcat", (128, SD * 528), BF16)
                gsum = sb2("gsum", (128, SD * 512), F32)
                zt = [sb2("zt%d" % i, (128, SD * 512), BF16) for i in range(2)]
                su = [sb2("su%d" % i, (128, SD * 512), BF16) for i in range(2)]
                gb = sb2("gb", (128, SD * 512), BF16)
                prt_ = sb2("prt", (128, SD * 512), BF16)
                ps2 = [C2.enter_context(nc.psum_tensor("p2ps%d" % i, [128, 512], F32)) for i in range(6)]

                b_wl = sem("b_wl")
                b_txx = sem("b_txx")
                b_ta = sem("b_ta")
                b_pe = sem("b_pe")
                b_ev = sem("b_ev")
                b_z = sem("b_z")
                b_sil = sem("b_sil")
                b_gad = sem("b_gad")
                b_gb = sem("b_gb")
                b_qs = sem("b_qs")
                b_qw = sem("b_qw")
                b_pw = sem("b_pw")

                GPE = 4 * SD
                HF = SD // 2
                HW = HF * 512
                zix = lambda g, w, h: 8 * g + (w - 1) + 4 * h + 1
                aix = lambda g, w, h: 6 * g + (w - 2) + 3 * h + 1
                block2 = C2.enter_context(nc.Block())

                @block2.sync
                def _(eng):
                    for (w, wi) in [(w_q, IWQ), (w_a1, IA1), (w_b1, IB1), (w_c, IWC), (w_bm, IBM)]:
                        eng.dma_start(out=w[:], in_=wpre[wi, :, :]).then_inc(b_wl, 16)
                    for g in range(NG):
                        eng.wait_ge(b_qs, 2 * g + 1)
                        eng.dma_start(out=q_scr[g, :, :], in_=qsb_[:]).then_inc(b_qw, 16)
                        eng.wait_ge(b_qs, 2 * g + 2)
                        eng.dma_start(out=part_scr[g, :, :], in_=prt_[:]).then_inc(b_pw, 16)

                @block2.scalar
                def _(eng):
                    def silu(g, w, h):
                        eng.wait_ge(b_z, zix(g, w, h))
                        if w == 1:
                            if g >= 1:
                                eng.wait_ge(b_gb, 2 * g)
                            ch_s(eng.activation(gsum[:, HW * h:HW * h + HW],
                                                zt[1][:, HW * h:HW * h + HW], AF.Silu))
                        else:
                            if w == 4:
                                eng.wait_ge(b_gad, aix(g, 2, h))
                            elif g >= 1:
                                eng.wait_ge(b_gad, aix(g - 1, 6 - w, h))
                            ch_s(eng.activation(su[w % 2][:, HW * h:HW * h + HW],
                                                zt[w % 2][:, HW * h:HW * h + HW], AF.Silu))
                        ch_s.signal(eng, b_sil, 1)

                    for g in range(NG):
                        if g >= 1:
                            eng.wait_ge(b_pe, GPE * (g - 1) + 3 * SD)
                        for i in range(SD):
                            eng.dma_start_transpose(out=xof_[:, 512 * i:512 * i + 512],
                                                    in_=xno_scr[512 * g:512 * g + 512, 128 * i:128 * i + 128]) \
                               .then_inc(b_txx, 16)
                        if g >= 1:
                            eng.wait_ge(b_pe, GPE * g)
                        for i in range(SD):
                            eng.dma_start_transpose(out=af_[:, 512 * i:512 * i + 512],
                                                    in_=a_scr[512 * g:512 * g + 512, 128 * i:128 * i + 128]) \
                               .then_inc(b_ta, 16)
                        for o in range(SD):
                            gg = GPE * g + o
                            eng.wait_ge(b_pe, gg + 1)
                            ch_s(eng.activation(ub[:, 512 * o:512 * o + 512], ps2[gg % 6][:, :],
                                                AF.Identity, bias=bcv(BM1, o)))
                            ch_s.signal(eng, b_ev, 1)
                        for o in range(SD):
                            gg = GPE * g + SD + o
                            eng.wait_ge(b_pe, gg + 1)
                            ch_s(eng.activation(wcat[:, 528 * o:528 * o + 528]
                                                .rearrange("p (j n) -> p j n", n=132)[:, :, 4:132],
                                                ps2[gg % 6][:, :].rearrange("p (j n) -> p j n", n=128),
                                                AF.Copy))
                            ch_s.signal(eng, b_ev, 1)
                        if g >= 1:
                            eng.wait_ge(b_qw, 16 * g)

                        def qevac(o):
                            gg = GPE * g + 2 * SD + o
                            eng.wait_ge(b_pe, gg + 1)
                            ch_s(eng.activation(qsb_[:, 512 * o:512 * o + 512], ps2[gg % 6][:, :],
                                                AF.Identity, bias=bcv(BQ, o), scale=kc(K32)))
                            ch_s.signal(eng, b_ev, 1)

                        qi = 0
                        for si, (w, h) in enumerate([(1, 0), (2, 0), (3, 0), (4, 0),
                                                     (1, 1), (2, 1), (3, 1), (4, 1)]):
                            silu(g, w, h)
                            if si >= 1 and qi < SD:
                                qevac(qi)
                                qi += 1
                        while qi < SD:
                            qevac(qi)
                            qi += 1
                        ch_s.signal(eng, b_qs, 1)
                        if g >= 1:
                            eng.wait_ge(b_pw, 16 * g)
                        for o in range(SD):
                            gg = GPE * g + 3 * SD + o
                            eng.wait_ge(b_pe, gg + 1)
                            ch_s(eng.activation(prt_[:, 512 * o:512 * o + 512], ps2[gg % 6][:, :],
                                                AF.Identity, bias=bcv(BOB, o)))
                            ch_s.signal(eng, b_ev, 1)
                        ch_s.signal(eng, b_qs, 1)

                @block2.vector
                def _(eng):
                    r4 = lambda t: t[:].rearrange("p (o j n) -> p o j n", j=4, n=128)
                    rw = lambda: wcat[:].rearrange("p (o j n) -> p o j n", j=4, n=132)

                    for g in range(NG):
                        ch_v(eng.tensor_copy(
                            rw()[:, :, :, 0:4],
                            whalo[:].rearrange("p (o t k) -> p o t k", t=32, k=4)[:, :, 4 * g:4 * g + 4, :]))

                        def zop(w, h):
                            if w == 1 and g >= 1:
                                eng.wait_ge(b_sil, zix(g - 1, 3, h))
                            if w == 2 and g >= 1:
                                eng.wait_ge(b_sil, zix(g - 1, 4, h))
                            if w == 3:
                                eng.wait_ge(b_sil, zix(g, 1, h))
                            if w == 4:
                                eng.wait_ge(b_sil, zix(g, 2, h))
                            if w == 1 and h == 0:
                                eng.wait_ge(b_ev, GPE * g + SD + HF)
                            if w == 1 and h == 1:
                                eng.wait_ge(b_ev, GPE * g + 2 * SD)
                            ch_v(eng.tensor_tensor(
                                r4(zt[w % 2])[:, HF * h:HF * h + HF],
                                r4(ub)[:, HF * h:HF * h + HF],
                                rw()[:, HF * h:HF * h + HF, :, 4 - w:132 - w],
                                op=OP.add))
                            ch_v.signal(eng, b_z, 1)

                        def gadd(w, h):
                            eng.wait_ge(b_sil, zix(g, w, h))
                            ch_v(eng.tensor_tensor(gsum[:, HW * h:HW * h + HW],
                                                   gsum[:, HW * h:HW * h + HW],
                                                   su[w % 2][:, HW * h:HW * h + HW], op=OP.add))
                            ch_v.signal(eng, b_gad, 1)

                        def gbcopy(h):
                            ch_v(eng.tensor_copy(gb[:, HW * h:HW * h + HW],
                                                 gsum[:, HW * h:HW * h + HW]))
                            ch_v.signal(eng, b_gb, 1)

                        zop(1, 0); zop(2, 0); zop(3, 0); zop(4, 0)
                        zop(1, 1); zop(2, 1)
                        if g >= 1:
                            eng.wait_ge(b_pe, GPE * (g - 1) + 4 * SD)
                        gadd(2, 0); gadd(3, 0); gadd(4, 0); gbcopy(0)
                        zop(3, 1); zop(4, 1)
                        gadd(2, 1); gadd(3, 1); gadd(4, 1); gbcopy(1)

                @block2.tensor
                def _(eng):
                    eng.wait_ge(b_wl, 16 * 5)
                    for g in range(NG):
                        eng.wait_ge(b_txx, 16 * SD * (g + 1))
                        for mat, w in ((0, w_a1), (1, w_b1), (2, w_q)):
                            for o in range(SD):
                                gg = GPE * g + SD * mat + o
                                if gg >= 6:
                                    eng.wait_ge(b_ev, gg - 5)
                                for i in range(SD):
                                    ins = eng.matmul(ps2[gg % 6][:, :],
                                                     w[:, D * i + 128 * o:D * i + 128 * o + 128],
                                                     xof_[:, 512 * i:512 * i + 512],
                                                     start=(i == 0), stop=(i == SD - 1))
                                    if i == SD - 1:
                                        ins.then_inc(b_pe, 1)
                        eng.wait_ge(b_gb, 2 * (g + 1))
                        eng.wait_ge(b_ta, 16 * SD * (g + 1))
                        for o in range(SD):
                            gg = GPE * g + 3 * SD + o
                            if gg >= 6:
                                eng.wait_ge(b_ev, gg - 5)
                            for i in range(SD):
                                eng.matmul(ps2[gg % 6][:, :],
                                           w_c[:, D * i + 128 * o:D * i + 128 * o + 128],
                                           gb[:, 512 * i:512 * i + 512],
                                           start=(i == 0), stop=False)
                            for i in range(SD):
                                ins = eng.matmul(ps2[gg % 6][:, :],
                                                 w_bm[:, D * i + 128 * o:D * i + 128 * o + 128],
                                                 af_[:, 512 * i:512 * i + 512],
                                                 start=False, stop=(i == SD - 1))
                                if i == SD - 1:
                                    ins.then_inc(b_pe, 1)

                barrier(block2, sync_waits=[(b_qw, 16 * NG), (b_pw, 16 * NG)])

        def phase2():
            C3 = ExitStack()
            with C3:
                _lsems = []
                def sem(n, _st=C3, _ls=_lsems):
                    h = _st.enter_context(nc.semaphore(n))
                    _ls.append(h)
                    return h
                sb3 = lambda n, sh, dt: C3.enter_context(nc.sbuf_tensor(n, list(sh), dt))
                qb = sb3("qb", (128, NG * SD * 512), BF16)
                kp = [sb3("kp%d" % i, (128, SD * 512), BF16) for i in range(3)]
                vb = sb3("vb", (128, 4 * TO * (TO + 1)), F32)
                ib = sb3("ib", (128, 4 * TO * (TO + 1)), F32)
                off = [4 * t * (t + 1) for t in range(TO + 1)]
                u32t = sb3("u32t", (128, 8), U32)
                f8 = sb3("f8", (128, 8), F32)
                e8 = sb3("e8", (128, 8), F32)
                p8 = [sb3("p8_%d" % i, (128, 8), F32) for i in range(2)]
                negm = sb3("negm", (128, 1), F32)
                lsum = sb3("lsum", (128, 1), F32)
                tmp8 = sb3("tmp8", (128, 8 * TO), F32)
                idm = sb3("idm", (128, 8), F32)
                idxi = sb3("idxi", (128, 8), I32)
                vsel = [sb3("vsel%d" % i, (128, 8 * D), BF16) for i in range(2)]
                gacc = [sb3("gacc%d" % i, (128, D), F32) for i in range(2)]
                gbf = [sb3("gbf%d" % i, (128, D), BF16) for i in range(2)]
                wtm = [sb3("wtm%d" % j, (128, D), BF16) for j in range(8)]
                ps3 = [C3.enter_context(nc.psum_tensor("p3ps%d" % i, [128, 512], F32)) for i in range(4)]

                c_ql = sem("c_ql")
                c_kp = [sem("c_kp0"), sem("c_kp1"), sem("c_kp2")]
                c_kf = sem("c_kf")
                c_un = sem("c_un")
                c_uv = sem("c_uv")
                c_f8 = sem("c_f8")
                c_ex = sem("c_ex")
                c_gt = sem("c_gt")
                c_gg = sem("c_gg")
                c_gw = [sem("c_gw0"), sem("c_gw1")]
                c_wm = sem("c_wm")
                c_wa = sem("c_wa")

                units = []
                for p in range(NP):
                    for j in range(p, TO):
                        units.append((p, j))
                uid = {pj: i for i, pj in enumerate(units)}
                NU = len(units)

                block3 = C3.enter_context(nc.Block())

                @block3.sync
                def _(eng):
                    for g in range(NG):
                        eng.dma_start(out=qb[:, SD * 512 * g:SD * 512 * (g + 1)],
                                      in_=q_scr[g, :, :]).then_inc(c_ql, 16)
                    def glob_write(t):
                        eng.wait_ge(c_gg, t + 1)
                        eng.dma_start(out=glob_scr[128 * t:128 * t + 128, :], in_=gbf[t % 2][:]) \
                           .then_inc(c_gw[t % 2], 16)

                    for p in range(NP):
                        if p >= 3:
                            eng.wait_ge(c_kf, p - 2)
                        eng.dma_start(out=kp[p % 3][:], in_=k_scr[p, :, :]).then_inc(c_kp[p % 3], 16)
                        if p >= 3:
                            glob_write(p - 3)
                    for t in range(max(0, TO - 3), TO):
                        glob_write(t)

                @block3.tensor
                def _(eng):
                    eng.wait_ge(c_ql, 16 * NG)
                    for p in range(NP):
                        eng.wait_ge(c_kp[p % 3], 16 * (p // 3 + 1))
                        for j in range(p, TO):
                            u = uid[(p, j)]
                            if u >= 4:
                                eng.wait_ge(c_uv, u - 3)
                            g4, j4 = divmod(j, 4)
                            for i in range(SD):
                                ins = eng.matmul(ps3[u % 4][:, :],
                                                 qb[:, SD * 512 * g4 + 512 * i + 128 * j4:
                                                     SD * 512 * g4 + 512 * i + 128 * j4 + 128],
                                                 kp[p % 3][:, 512 * i:512 * i + 512],
                                                 start=(i == 0), stop=(i == SD - 1))
                                if i == SD - 1:
                                    ins.then_inc(c_un, 1)
                        i = eng.sem_inc(c_kf, 1)
                        i._wait_ge(c_un, uid[(p, TO - 1)] + 1)

                @block3.vector
                def _(eng):
                    def wsum(tw):
                        if tw >= 2:
                            eng.wait_ge(c_gw[tw % 2], 16 * (tw // 2))
                        for j in range(8):
                            eng.wait_ge(c_wm, 8 * tw + j + 1)
                            if j == 0:
                                ch_v(eng.tensor_copy(gacc[tw % 2][:], wtm[0][:]))
                            elif j < 7:
                                ch_v(eng.tensor_tensor(gacc[tw % 2][:], gacc[tw % 2][:],
                                                       wtm[j][:], op=OP.add))
                            else:
                                ch_v(eng.tensor_tensor(gbf[tw % 2][:], gacc[tw % 2][:],
                                                       wtm[7][:], op=OP.add))
                            ch_v.signal(eng, c_wa, 1)
                        ch_v.signal(eng, c_gg, 1)

                    for p in range(NP):
                        for j in range(p, TO):
                            u = uid[(p, j)]
                            eng.wait_ge(c_un, u + 1)
                            if j == p:
                                ch_v(eng.tensor_tensor(ps3[u % 4][:, :], ps3[u % 4][:, :],
                                                       dmask[:], op=OP.add))
                            ch_v(eng.max(out=vb[:, off[j] + 8 * p:off[j] + 8 * p + 8],
                                         in_=ps3[u % 4][:, :]))
                            ch_v(eng.max_index(out=u32t[:], in_max=vb[:, off[j] + 8 * p:off[j] + 8 * p + 8],
                                               in_values=ps3[u % 4][:, :]))
                            ch_v(eng.tensor_scalar(ib[:, off[j] + 8 * p:off[j] + 8 * p + 8],
                                                   u32t[:], float(512 * p - BIGI), None, op0=OP.add))
                            ch_v.signal(eng, c_uv, 1)
                        if p >= 2:
                            wsum(p - 2)
                        t = p
                        if t < TO:
                            W8 = 8 * (t + 1)
                            eng.wait_ge(c_gt, 128 * t)
                            ch_v(eng.max(out=f8[:], in_=vb[:, off[t]:off[t] + W8]))
                            ch_v(eng.tensor_scalar(negm[:, 0:1], f8[:, 0:1], -1.0, None, op0=OP.mult))
                            ch_v.signal(eng, c_f8, 1)
                            eng.wait_ge(c_ex, t + 1)
                            ch_v(eng.reduce_sum(lsum[:, 0:1], e8[:], axis=AX))
                            ch_v(eng.reciprocal(lsum[:, 0:1], lsum[:, 0:1]))
                            ch_v(eng.tensor_scalar(p8[t % 2][:], e8[:], lsum[:, 0:1], None, op0=OP.mult))
                            for k in range(8):
                                ch_v(eng.scalar_tensor_tensor(tmp8[:, 0:W8], vb[:, off[t]:off[t] + W8],
                                                              f8[:, k:k + 1], ib[:, off[t]:off[t] + W8],
                                                              op0=OP.is_equal, op1=OP.mult))
                                ch_v(eng.tensor_reduce(idm[:, k:k + 1], tmp8[:, 0:W8], axis=AX, op=OP.min))
                            ch_v(eng.tensor_scalar(idm[:], idm[:], BIGI, None, op0=OP.add))
                            ch_v(eng.tensor_copy(idxi[:], idm[:]))
                            ch_v.signal(eng, c_f8, 1)
                    if TO >= 2:
                        wsum(TO - 2)
                    wsum(TO - 1)

                @block3.scalar
                def _(eng):
                    for t in range(TO):
                        eng.wait_ge(c_f8, 2 * t + 1)
                        if t >= 1:
                            eng.wait_ge(c_f8, 2 * t)
                        ch_s(eng.activation(e8[:], f8[:], AF.Exp, bias=negm[:, 0:1]))
                        ch_s.signal(eng, c_ex, 1)
                        eng.wait_ge(c_f8, 2 * t + 2)
                        eng.wait_ge(c_gt, 128 * (t + 1))
                        for j in range(8):
                            if t >= 1:
                                eng.wait_ge(c_wa, 8 * (t - 1) + j + 1)
                            ch_s(eng.activation(wtm[j][:], vsel[t % 2][:, D * j:D * j + D],
                                                AF.Identity, bias=kc(KZ), scale=p8[t % 2][:, j:j + 1]))
                            ch_s.signal(eng, c_wm, 1)

                @block3.gpsimd
                def _(eng):
                    for t in range(TO):
                        eng.wait_ge(c_f8, 2 * t + 2)
                        if t >= 2:
                            eng.wait_ge(c_wm, 8 * (t - 1))
                        for j in range(8):
                            ins = eng.indirect_dma_start(
                                out=vsel[t % 2][:, D * j:D * j + D], out_offset=None,
                                in_=v_scr[:, :], in_offset=IndirectOffsetOnAxis(idxi[:, j:j + 1], 0))
                            ins.then_inc(c_gt, 16)

                barrier(block3, sync_waits=[(c_gw[0], 16 * (TO // 2)), (c_gw[1], 16 * (TO // 2))])

        def phase3():
            C4 = ExitStack()
            with C4:
                _lsems = []
                def sem(n, _st=C4, _ls=_lsems):
                    h = _st.enter_context(nc.semaphore(n))
                    _ls.append(h)
                    return h
                sb4 = lambda n, sh, dt: C4.enter_context(nc.sbuf_tensor(n, list(sh), dt))
                w_o2 = sb4("w_o24", (128, SD * D), BF16)
                gf = [sb4("gf%d" % i, (128, SD * 512), BF16) for i in range(2)]
                pb = [sb4("pb%d" % i, (128, SD * 512), BF16) for i in range(2)]
                res = [sb4("res%d" % i, (128, SD * 512), BF16) for i in range(2)]
                rt = [sb4("rt%d" % i, (128, 4 * D), BF16) for i in range(2)]
                xo4 = [sb4("xo4%d" % i, (128, 4 * D), F32) for i in range(2)]
                ob = [sb4("ob%d" % i, (128, 4 * D), F32) for i in range(2)]
                ps4 = [C4.enter_context(nc.psum_tensor("p4ps%d" % i, [128, 512], F32)) for i in range(4)]

                d_wl = sem("d_wl")
                d_tx = [sem("d_tx0"), sem("d_tx1")]
                d_pl = [sem("d_pl0"), sem("d_pl1")]
                d_xl = [sem("d_xl0"), sem("d_xl1")]
                d_pe = sem("d_pe")
                d_evo = sem("d_evo")
                d_res = sem("d_res")
                d_ad = sem("d_ad")
                d_rw = [sem("d_rw0"), sem("d_rw1")]
                d_rt = [sem("d_rt0"), sem("d_rt1")]
                d_ow = [sem("d_ow0"), sem("d_ow1")]

                block4 = C4.enter_context(nc.Block())

                @block4.sync
                def _(eng):
                    eng.dma_start(out=w_o2[:], in_=wpre[IO2, :, :]).then_inc(d_wl, 16)
                    def out_write(q):
                        eng.wait_ge(d_ad, q + 1)
                        eng.dma_start(out=out_d[512 * q:512 * q + 512, :].rearrange("(t p) d -> p t d", p=128),
                                      in_=ob[q % 2][:].rearrange("p (t d) -> p t d", d=D)) \
                           .then_inc(d_ow[q % 2], 16)
                    for g in range(NG):
                        s = g % 2
                        if g >= 2:
                            eng.wait_ge(d_res, g - 1)
                        eng.dma_start(out=pb[s][:], in_=part_scr[g, :, :]).then_inc(d_pl[s], 16)
                        if g >= 2:
                            eng.wait_ge(d_ad, g - 1)
                        eng.dma_start(out=xo4[s][:].rearrange("p (t d) -> p t d", d=D),
                                      in_=xo[512 * g:512 * g + 512, :].rearrange("(t p) d -> p t d", p=128)) \
                           .then_inc(d_xl[s], 16)
                        eng.wait_ge(d_res, g + 1)
                        for o in range(SD):
                            eng.dma_start(out=res_scr[g, 128 * o:128 * o + 128, :],
                                          in_=res[s][:, 512 * o:512 * o + 512]).then_inc(d_rw[s], 16)
                        if g >= 1:
                            out_write(g - 1)
                    out_write(NG - 1)
                    eng.wait_ge(d_ow[0], 16 * ((NG + 1) // 2))
                    if NG >= 2:
                        eng.wait_ge(d_ow[1], 16 * (NG // 2))

                @block4.scalar
                def _(eng):
                    def rt_tx(q):
                        if q >= 2:
                            eng.wait_ge(d_ad, q - 1)
                        eng.wait_ge(d_rw[q % 2], 16 * SD * (q // 2 + 1))
                        for t in range(4):
                            eng.dma_start_transpose(out=rt[q % 2][:, D * t:D * t + D],
                                                    in_=res_scr[q, :, 128 * t:128 * t + 128]) \
                               .then_inc(d_rt[q % 2], 16)
                    for g in range(NG):
                        s = g % 2
                        if g >= 2:
                            eng.wait_ge(d_res, g - 1)
                        for i in range(SD):
                            eng.dma_start_transpose(out=gf[s][:, 512 * i:512 * i + 512],
                                                    in_=glob_scr[512 * g:512 * g + 512, 128 * i:128 * i + 128]) \
                               .then_inc(d_tx[s], 16)
                        if g >= 1:
                            rt_tx(g - 1)
                    rt_tx(NG - 1)

                @block4.vector
                def _(eng):
                    def xadd(q):
                        if q >= 2:
                            eng.wait_ge(d_ow[q % 2], 16 * (q // 2))
                        eng.wait_ge(d_rt[q % 2], 16 * 4 * (q // 2 + 1))
                        eng.wait_ge(d_xl[q % 2], 16 * (q // 2 + 1))
                        for t in range(4):
                            ch_v(eng.tensor_tensor(ob[q % 2][:, D * t:D * t + D], rt[q % 2][:, D * t:D * t + D],
                                                   xo4[q % 2][:, D * t:D * t + D], op=OP.add))
                        ch_v.signal(eng, d_ad, 1)
                    for g in range(NG):
                        s = g % 2
                        if g >= 2:
                            eng.wait_ge(d_rw[s], 16 * SD * (g // 2))
                        for o in range(SD):
                            gg = SD * g + o
                            eng.wait_ge(d_pe, gg + 1)
                            if o == 0:
                                eng.wait_ge(d_pl[s], 16 * (g // 2 + 1))
                            ch_v(eng.tensor_tensor(res[s][:, 512 * o:512 * o + 512], ps4[gg % 4][:, :],
                                                   pb[s][:, 512 * o:512 * o + 512], op=OP.add))
                            ch_v.signal(eng, d_evo, 1)
                        ch_v.signal(eng, d_res, 1)
                        if g >= 1:
                            xadd(g - 1)
                    xadd(NG - 1)

                @block4.tensor
                def _(eng):
                    eng.wait_ge(d_wl, 16)
                    for g in range(NG):
                        s = g % 2
                        eng.wait_ge(d_tx[s], 16 * SD * (g // 2 + 1))
                        for o in range(SD):
                            gg = SD * g + o
                            if gg >= 4:
                                eng.wait_ge(d_evo, gg - 3)
                            for i in range(SD):
                                ins = eng.matmul(ps4[gg % 4][:, :],
                                                 w_o2[:, D * i + 128 * o:D * i + 128 * o + 128],
                                                 gf[s][:, 512 * i:512 * i + 512],
                                                 start=(i == 0), stop=(i == SD - 1))
                                if i == SD - 1:
                                    ins.then_inc(d_pe, 1)

                barrier(block4, sync_waits=[(d_ow[0], 16 * ((NG + 1) // 2))]
                        + ([(d_ow[1], 16 * (NG // 2))] if NG >= 2 else []))

        phase0()
        phase1a()
        phase1b()
        phase2()
        phase3()
    return nc


_CACHE = {}


def _get_prog(cfg_key):
    if cfg_key not in _CACHE:
        _CACHE[cfg_key] = build_program(CFG(*cfg_key))
    return _CACHE[cfg_key]


def kernel(L0_attn_mu, L0_attn_Wq, L0_attn_bq, L0_attn_Wk, L0_attn_bk,
           L0_attn_Wv, L0_attn_bv, L0_attn_Wm1, L0_attn_bm1, L0_attn_Wm2,
           L0_attn_bm2, L0_attn_Wo, L0_attn_bo, L0_ffn_weight,
           L0_ffn_weight_scale, L0_ffn_bias):
    import ml_dtypes

    x = np.ascontiguousarray(np.asarray(L0_attn_mu, dtype=np.float32))
    B, N, D = x.shape
    c = CFG(D=D, N=N)
    nc = _get_prog((D, N))

    f32 = lambda a: np.ascontiguousarray(np.asarray(a, dtype=np.float32))
    w8 = np.asarray(L0_ffn_weight)
    w8b = np.ascontiguousarray(w8.view(np.uint8))
    ws = np.full((128, 1), np.float32(np.asarray(L0_ffn_weight_scale)), dtype=np.float32)

    shared = dict(
        wq=f32(L0_attn_Wq), wk=f32(L0_attn_Wk), wv=f32(L0_attn_Wv),
        bq=f32(L0_attn_bq), bk=f32(L0_attn_bk), bv=f32(L0_attn_bv),
        wm1=f32(L0_attn_Wm1), bm1=f32(L0_attn_bm1), wm2=f32(L0_attn_Wm2),
        bm2=f32(L0_attn_bm2), wo=f32(L0_attn_Wo), bo=f32(L0_attn_bo),
        fw8=w8b, fws=ws, fb=f32(L0_ffn_bias),
    )

    in_maps = []
    for core in range(8):
        b, r = core // 4, core % 4
        xb_c = x[b]
        tbs = [4 * j + r for j in range(c.TO)]
        xo_c = np.concatenate([xb_c[128 * tb:128 * tb + 128] for tb in tbs], 0)
        halo = np.zeros((128, D), dtype=np.float32)
        for j, tb in enumerate(tbs):
            if tb > 0:
                halo[4 * j:4 * j + 4] = xb_c[128 * tb - 4:128 * tb]
        dm = np.zeros((128, 512), dtype=np.float32)
        cols = np.arange(512)[None, :]
        rows = np.arange(128)[:, None]
        dm[cols > 128 * r + rows] = NEG
        m = dict(shared)
        m.update(xb=np.ascontiguousarray(xb_c), xo=np.ascontiguousarray(xo_c),
                 xh=halo, dmask=dm)
        in_maps.append(m)

    res = run_bass_kernel_spmd(nc, in_maps, core_ids=list(range(8)))
    out = np.empty((B, N, D), dtype=np.float32)
    for core in range(8):
        b, r = core // 4, core % 4
        oc = np.asarray(res.results[core]["out"])
        for j in range(c.TO):
            tb = 4 * j + r
            out[b, 128 * tb:128 * tb + 128] = oc[128 * j:128 * j + 128]
    return out


# version 27
# speedup vs baseline: 1.0169x; 1.0113x over previous
import numpy as np
from contextlib import ExitStack

import concourse.bass as bass
import concourse.mybir as mybir
from concourse.alu_op_type import AluOpType as OP
from concourse.bass_core_rust import IndirectOffsetOnAxis
from concourse.bass_utils import run_bass_kernel_spmd

AF = mybir.ActivationFunctionType
AX = mybir.AxisListType.X
F32 = mybir.dt.float32
BF16 = mybir.dt.bfloat16
I32 = mybir.dt.int32
U32 = mybir.dt.uint32
U8 = mybir.dt.uint8
FP8 = mybir.dt.float8e4

NEG = -1.0e30
BIGI = 65536.0


class CFG:
    def __init__(self, D=1024, N=16384):
        self.D = D
        self.SD = D // 128
        self.N = N
        self.T = N // 128
        self.TO = self.T // 4
        self.NP = N // 512
        self.NG = self.TO // 4
        assert self.TO % 4 == 0 and self.SD >= 2 and self.NP >= 2 and self.NG >= 1


class Chain:

    def __init__(self, nc, ctx, name):
        self.s = ctx.enter_context(nc.semaphore(name))
        self.n = 0

    def __call__(self, instr, w=True):
        if w and self.n > 0:
            instr._wait_ge(self.s, self.n)
        instr.then_inc(self.s, 1)
        self.n += 1
        return self.n

    def signal(self, eng, sem, k=1):
        i = eng.sem_inc(sem, k)
        if self.n > 0:
            i._wait_ge(self.s, self.n)
        return i


def build_program(c: CFG):
    nc = bass.Bass("TRN2", target_bir_lowering=False, debug=False)
    D, N, SD, TO, NP, NG = c.D, c.N, c.SD, c.TO, c.NP, c.NG

    dr = lambda name, shape, dt, kind="Internal": nc.dram_tensor(name, list(shape), dt, kind=kind)
    xb = dr("xb", (N, D), F32, "ExternalInput")
    xo = dr("xo", (TO * 128, D), F32, "ExternalInput")
    xh = dr("xh", (128, D), F32, "ExternalInput")
    wq_d = dr("wq", (D, D), F32, "ExternalInput")
    wk_d = dr("wk", (D, D), F32, "ExternalInput")
    wv_d = dr("wv", (D, D), F32, "ExternalInput")
    bq_d = dr("bq", (D,), F32, "ExternalInput")
    bk_d = dr("bk", (D,), F32, "ExternalInput")
    bv_d = dr("bv", (D,), F32, "ExternalInput")
    wm1_d = dr("wm1", (2 * D, D), F32, "ExternalInput")
    bm1_d = dr("bm1", (D,), F32, "ExternalInput")
    wm2_d = dr("wm2", (D, D), F32, "ExternalInput")
    bm2_d = dr("bm2", (D,), F32, "ExternalInput")
    wo_d = dr("wo", (2 * D, D), F32, "ExternalInput")
    bo_d = dr("bo", (D,), F32, "ExternalInput")
    fw8_d = dr("fw8", (D, D), U8, "ExternalInput")
    fws_d = dr("fws", (128, 1), F32, "ExternalInput")
    fb_d = dr("fb", (D,), F32, "ExternalInput")
    dmask_d = dr("dmask", (128, 512), F32, "ExternalInput")
    out_d = dr("out", (TO * 128, D), F32, "ExternalOutput")
    xn_scr = dr("xn_scr", (N, D), BF16)
    k_scr = dr("k_scr", (NP, 128, SD * 512), BF16)
    vfm_scr = dr("vfm_scr", (NP, D, 512), BF16)
    v_scr = dr("v_scr", (N, D), BF16)
    xno_scr = dr("xno_scr", (TO * 128, D), BF16)
    a_scr = dr("a_scr", (TO * 128, D), BF16)
    xh_scr = dr("xh_scr", (128, D), BF16)
    q_scr = dr("q_scr", (NG, 128, SD * 512), BF16)
    w2t_scr = dr("w2t_scr", (D, D), BF16)
    bmt_scr = dr("bmt_scr", (D, D), BF16)
    part_scr = dr("part_scr", (NG, 128, SD * 512), BF16)
    glob_scr = dr("glob_scr", (TO * 128, D), BF16)
    res_scr = dr("res_scr", (NG, D, 512), BF16)
    wpre = dr("wpre", (8, 128, SD * D), BF16)
    IWQ, IWK, IWV, IA1, IB1, IO2, IWC, IBM = range(8)

    QSC = float(1.0 / np.sqrt(D))
    PS = ExitStack()
    with PS:
        sbp = lambda n, sh, dt: PS.enter_context(nc.sbuf_tensor(n, list(sh), dt))
        sem = lambda n: PS.enter_context(nc.semaphore(n))

        bias = sbp("bias", (128, 64), F32)
        BQ, BK, BV, BM1, BOB, BO, FB, BM2 = range(8)
        bcv = lambda slot, o: bias[:, 8 * slot + o:8 * slot + o + 1]
        dmask = sbp("dmask_sb", (128, 512), F32)
        fws_sb = sbp("fws_sb", (128, 1), F32)
        whalo = sbp("whalo", (128, SD * 128), F32)
        kcn = sbp("kcn", (128, 16), F32)
        KEPS, KQRT, K32, K12, KZ = 0, 1, 2, 3, 15
        kc = lambda i: kcn[:, i:i + 1]

        ch_v = Chain(nc, PS, "ch_v")
        ch_s = Chain(nc, PS, "ch_s")
        ch_g = Chain(nc, PS, "ch_g")
        s_bar = sem("s_bar")
        bar = [0]

        def barrier(block, sync_waits=(), pe_wait=None, clear=()):
            base = bar[0]
            bar[0] += 6 if clear else 5

            @block.sync
            def _(eng):
                for (s, v) in sync_waits:
                    eng.wait_ge(s, v)
                eng.sem_inc(s_bar, 1)
                eng.wait_ge(s_bar, base + 5)
                if clear:
                    for h in clear:
                        eng.sem_clear(h)
                    eng.sem_inc(s_bar, 1)

            @block.scalar
            def _(eng):
                ch_s.signal(eng, s_bar, 1)
                eng.wait_ge(s_bar, base + (6 if clear else 5))

            @block.vector
            def _(eng):
                ch_v.signal(eng, s_bar, 1)
                eng.wait_ge(s_bar, base + (6 if clear else 5))

            @block.gpsimd
            def _(eng):
                ch_g.signal(eng, s_bar, 1)
                eng.wait_ge(s_bar, base + (6 if clear else 5))

            @block.tensor
            def _(eng):
                i = eng.sem_inc(s_bar, 1)
                if pe_wait is not None:
                    i._wait_ge(pe_wait[0], pe_wait[1])
                eng.wait_ge(s_bar, base + (6 if clear else 5))

        def phase0():
            C0 = ExitStack()
            with C0:
                _lsems = []
                def sem(n, _st=C0, _ls=_lsems):
                    h = _st.enter_context(nc.semaphore(n))
                    _ls.append(h)
                    return h
                sb0 = lambda n, sh, dt: C0.enter_context(nc.sbuf_tensor(n, list(sh), dt))
                stg = [sb0("p0stg%d" % i, (128, D), F32) for i in range(6)]
                bcast = [sb0("p0bc%d" % i, (128, D), BF16) for i in range(2)]
                w_o1 = sb0("p0wo1", (128, SD * D), BF16)
                w_b1 = sb0("p0wb1", (128, SD * D), BF16)
                wm2tT = sb0("p0wm2tT", (128, SD * D), BF16)
                wbmT = sb0("p0wbmT", (128, SD * D), BF16)
                w_c0 = sb0("p0wc0", (128, SD * D), BF16)
                u8s = sb0("p0u8", (128, D), U8)
                mk8 = sb0("p0mk8", (128, D), U8)
                b28 = sb0("p0b28", (128, D), U8)
                b38 = sb0("p0b38", (128, D), U8)
                f32a = sb0("p0f32a", (128, D), F32)
                f32b = sb0("p0f32b", (128, D), F32)
                f32c = sb0("p0f32c", (128, D), F32)
                i32a = sb0("p0i32a", (128, D), I32)
                dect = sb0("p0dect", (128, D), BF16)
                xhx = sb0("p0xhx", (128, D), F32)
                xhsq = sb0("p0xhsq", (128, D), F32)
                xhb = sb0("p0xhb", (128, D), BF16)
                xhf = sb0("p0xhf", (128, SD * 128), BF16)
                bm2b = sb0("p0bm2b", (128, 8), BF16)
                ps0 = [C0.enter_context(nc.psum_tensor("p0ps%d" % i, [128, 512], F32)) for i in range(2)]
                psd = C0.enter_context(nc.psum_tensor("p0psd", [128, 8], F32))

                z_ld = [sem("z_ld%d" % i) for i in range(6)]
                z_u8 = sem("z_u8")
                z_sm = sem("z_sm")
                z_wp = [sem("z_wp0"), sem("z_wp1")]
                z_wb = sem("z_wb")
                z_wxh = sem("z_wxh")
                z_wcm = sem("z_wcm")
                z_tm2 = sem("z_tm2")
                z_tbm = sem("z_tbm")
                z_txh = sem("z_txh")
                z_v = sem("z_v")
                z_s = sem("z_s")
                z_pe = sem("z_pe")
                z_ev = sem("z_ev")

                casts = []
                for wi, (src, r0) in [(IWQ, (wq_d, 0)), (IWK, (wk_d, 0)), (IWV, (wv_d, 0)),
                                      (IA1, (wm1_d, 0)), (IO2, (wo_d, D))]:
                    for i in range(SD):
                        casts.append((src, r0 + 128 * i, "w", (wi, i)))
                for i in range(SD):
                    casts.append((wm1_d, D + 128 * i, "k", (w_b1, i)))
                for i in range(SD):
                    casts.append((wo_d, 128 * i, "k", (w_o1, i)))
                for i in range(SD):
                    casts.append((wm2_d, 128 * i, "s", i))
                NCAST = len(casts)
                slot_wr = {0: 0, 1: 0}
                wr_prior = []
                nws = {0: 0, 1: 0}
                for n, (src, r0, kind, arg) in enumerate(casts):
                    if kind in ("w", "s"):
                        wr_prior.append(nws[n % 2])
                        nws[n % 2] += 1
                    else:
                        wr_prior.append(None)
                NWR0, NWR1 = nws[0], nws[1]
                NH = max(1, D // 512)
                WH = min(D, 512)
                NGC = SD * NH
                E_CAST = NCAST
                E_DEC = E_CAST + SD
                E_HALOV = E_DEC + 1
                E_HALOB = E_HALOV + 1
                E_BM2B = E_HALOB + 1

                block0 = C0.enter_context(nc.Block())

                @block0.vector
                def _(eng):
                    for (slot, val) in [(KEPS, 1e-5), (KQRT, 0.25), (K32, QSC), (K12, 1e-12),
                                        (KZ, 0.0)]:
                        ch_v(eng.memset(kc(slot), val))
                    for n, (src, r0, kind, arg) in enumerate(casts):
                        eng.wait_ge(z_ld[n % 6], 16 * (n // 6 + 1))
                        if kind in ("w", "s"):
                            if wr_prior[n]:
                                eng.wait_ge(z_wp[n % 2], 16 * wr_prior[n])
                            ch_v(eng.tensor_copy(bcast[n % 2][:], stg[n % 6][:]))
                        else:
                            dst, i = arg
                            ch_v(eng.tensor_copy(dst[:, D * i:D * i + D], stg[n % 6][:]))
                        ch_v.signal(eng, z_v, 1)
                    for i in range(SD):
                        eng.wait_ge(z_u8, 16 * (i + 1))
                        if i >= 1:
                            eng.wait_ge(z_wb, 16 * i)
                        u = u8s[:]
                        ch_v(eng.tensor_scalar(mk8[:], u, 0x78, None, op0=OP.bitwise_and))
                        ch_v(eng.tensor_scalar(mk8[:], mk8[:], 0x78, None, op0=OP.is_equal))
                        ch_v(eng.tensor_scalar(b28[:], u, 0xF7, None, op0=OP.bitwise_and))
                        ch_v(eng.tensor_copy(b38[:], u))
                        ch_v(eng.copy_predicated(b38[:], mk8[:], b28[:]))
                        ch_v(eng.tensor_copy(f32c[:], b38[:].bitcast(FP8)))
                        ch_v(eng.tensor_copy(f32a[:], mk8[:]))
                        ch_v(eng.tensor_scalar(f32a[:], f32a[:], 1.0, None, op0=OP.add))
                        ch_v(eng.tensor_tensor(dect[:], f32c[:], f32a[:], op=OP.mult))
                        ch_v.signal(eng, z_v, 1)
                    eng.wait_ge(z_sm, 16 * 10)
                    ch_v(eng.reduce_sum(kcn[:, 4:5], xhx[:], axis=AX))
                    ch_v(eng.tensor_scalar(kcn[:, 4:5], kcn[:, 4:5], -1.0 / D, None, op0=OP.mult))
                    ch_v(eng.tensor_scalar(xhx[:], xhx[:], kcn[:, 4:5], None, op0=OP.add))
                    ch_v(eng.tensor_tensor(xhsq[:], xhx[:], xhx[:], op=OP.mult))
                    ch_v(eng.reduce_sum(kcn[:, 5:6], xhsq[:], axis=AX))
                    ch_v(eng.tensor_scalar(kcn[:, 5:6], kcn[:, 5:6], 1.0 / D, None, op0=OP.mult))
                    ch_v.signal(eng, z_v, 1)
                    eng.wait_ge(z_s, 1)
                    ch_v(eng.reciprocal(kcn[:, 6:7], kcn[:, 7:8]))
                    ch_v(eng.tensor_scalar(xhb[:], xhx[:], kcn[:, 6:7], None, op0=OP.mult))
                    ch_v.signal(eng, z_v, 1)
                    ch_v(eng.tensor_scalar(bias[:, 8 * BQ:8 * BQ + SD], bias[:, 8 * BQ:8 * BQ + SD],
                                           QSC, None, op0=OP.mult))
                    ch_v(eng.tensor_copy(bm2b[:, 0:SD], bias[:, 8 * BM2:8 * BM2 + SD]))
                    ch_v.signal(eng, z_v, 1)
                    for g in range(NGC):
                        isub, jh = divmod(g, NH)
                        eng.wait_ge(z_pe, g + 1)
                        ch_v(eng.tensor_scalar(w_c0[:, D * isub + WH * jh:D * isub + WH * jh + WH],
                                               ps0[g % 2][:, 0:WH], kc(KQRT), None, op0=OP.mult))
                        ch_v.signal(eng, z_ev, 1)
                    for g in range(SD):
                        eng.wait_ge(z_pe, NGC + g + 1)
                        ch_v(eng.tensor_copy(whalo[:, 128 * g:128 * g + 128], ps0[g % 2][:, 0:128]))
                        ch_v.signal(eng, z_ev, 1)
                    eng.wait_ge(z_pe, NGC + SD + 1)
                    ch_v(eng.tensor_tensor(bias[:, 8 * BOB:8 * BOB + SD], psd[:, 0:SD],
                                           bias[:, 8 * BO:8 * BO + SD], op=OP.add))
                    ch_v(eng.tensor_tensor(bias[:, 8 * BOB:8 * BOB + SD], bias[:, 8 * BOB:8 * BOB + SD],
                                           bias[:, 8 * FB:8 * FB + SD], op=OP.add))
                    ch_v.signal(eng, z_ev, 1)

                @block0.sync
                def _(eng):
                    def bcast_write(n):
                        (src, r0, kind, arg) = casts[n]
                        if kind == "w":
                            wi, i = arg
                            eng.wait_ge(z_v, n + 1)
                            eng.dma_start(out=wpre[wi, :, D * i:D * i + D],
                                          in_=bcast[n % 2][:]).then_inc(z_wp[n % 2], 16)
                        elif kind == "s":
                            i = arg
                            eng.wait_ge(z_v, n + 1)
                            eng.dma_start(out=w2t_scr[128 * i:128 * i + 128, :],
                                          in_=bcast[n % 2][:]).then_inc(z_wp[n % 2], 16)

                    for n, (src, r0, kind, arg) in enumerate(casts):
                        if n >= 6:
                            eng.wait_ge(z_v, n - 5)
                        eng.dma_start(out=stg[n % 6][:], in_=src[r0:r0 + 128, :]).then_inc(z_ld[n % 6], 16)
                        if n >= 2:
                            bcast_write(n - 2)
                    bcast_write(NCAST - 2)
                    bcast_write(NCAST - 1)
                    for i in range(SD):
                        if i >= 1:
                            eng.wait_ge(z_v, E_CAST + i)
                        eng.dma_start(out=u8s[:], in_=fw8_d[128 * i:128 * i + 128, :]).then_inc(z_u8, 16)
                        if i >= 1:
                            eng.wait_ge(z_v, E_CAST + i)
                            eng.dma_start(out=bmt_scr[128 * (i - 1):128 * (i - 1) + 128, :],
                                          in_=dect[:]).then_inc(z_wb, 16)
                    eng.wait_ge(z_v, E_CAST + SD)
                    eng.dma_start(out=bmt_scr[128 * (SD - 1):128 * SD, :], in_=dect[:]).then_inc(z_wb, 16)
                    for (slot, src) in [(BQ, bq_d), (BK, bk_d), (BV, bv_d), (BM1, bm1_d),
                                        (BO, bo_d), (FB, fb_d), (BM2, bm2_d)]:
                        eng.dma_start(out=bias[:, 8 * slot:8 * slot + SD],
                                      in_=src[:].rearrange("(j p) -> p j", p=128),
                                      allow_slow_non_contiguous=True).then_inc(z_sm, 16)
                    eng.dma_start(out=dmask[:], in_=dmask_d[:, :]).then_inc(z_sm, 16)
                    eng.dma_start(out=fws_sb[:, 0:1], in_=fws_d[:, 0:1]).then_inc(z_sm, 16)
                    eng.dma_start(out=xhx[:], in_=xh[:, :]).then_inc(z_sm, 16)
                    eng.wait_ge(z_v, E_CAST)
                    eng.dma_start(out=wpre[IB1, :, :], in_=w_b1[:, :]).then_inc(z_wcm, 16)
                    eng.wait_ge(z_v, E_HALOB)
                    eng.dma_start(out=xh_scr[:, :], in_=xhb[:]).then_inc(z_wxh, 16)
                    eng.wait_ge(z_ev, NGC)
                    eng.dma_start(out=wpre[IWC, :, :], in_=w_c0[:, :]).then_inc(z_wcm, 16)
                    eng.wait_ge(z_tbm, 16 * SD)
                    eng.dma_start(out=wpre[IBM, :, :], in_=wbmT[:, :]).then_inc(z_wcm, 16)

                @block0.scalar
                def _(eng):
                    eng.wait_ge(z_v, E_HALOV)
                    ch_s(eng.activation(kcn[:, 7:8], kcn[:, 5:6], AF.Sqrt, bias=kc(KEPS)))
                    ch_s.signal(eng, z_s, 1)
                    eng.wait_ge(z_wp[0], 16 * NWR0)
                    eng.wait_ge(z_wp[1], 16 * NWR1)
                    for m in range(SD):
                        eng.dma_start_transpose(out=wm2tT[:, D * m:D * m + D],
                                                in_=w2t_scr[:, 128 * m:128 * m + 128]).then_inc(z_tm2, 16)
                    eng.wait_ge(z_wb, 16 * SD)
                    for k in range(SD):
                        eng.dma_start_transpose(out=wbmT[:, D * k:D * k + D],
                                                in_=bmt_scr[:, 128 * k:128 * k + 128]).then_inc(z_tbm, 16)
                    eng.wait_ge(z_wxh, 16)
                    for i in range(SD):
                        eng.dma_start_transpose(out=xhf[:, 128 * i:128 * i + 128],
                                                in_=xh_scr[:, 128 * i:128 * i + 128]).then_inc(z_txh, 16)

                @block0.tensor
                def _(eng):
                    eng.wait_ge(z_v, E_CAST)
                    eng.wait_ge(z_tm2, 16 * SD)
                    g = 0
                    for isub in range(SD):
                        for jh in range(NH):
                            if g >= 2:
                                eng.wait_ge(z_ev, g - 1)
                            for m in range(SD):
                                ins = eng.matmul(ps0[g % 2][:, 0:WH],
                                                 wm2tT[:, D * m + 128 * isub:D * m + 128 * isub + 128],
                                                 w_o1[:, D * m + WH * jh:D * m + WH * jh + WH],
                                                 start=(m == 0), stop=(m == SD - 1))
                                if m == SD - 1:
                                    ins.then_inc(z_pe, 1)
                            g += 1
                    eng.wait_ge(z_txh, 16 * SD)
                    for o in range(SD):
                        gg = NGC + o
                        eng.wait_ge(z_ev, gg - 1)
                        for i in range(SD):
                            ins = eng.matmul(ps0[gg % 2][:, 0:128],
                                             w_b1[:, D * i + 128 * o:D * i + 128 * o + 128],
                                             xhf[:, 128 * i:128 * i + 128],
                                             start=(i == 0), stop=(i == SD - 1))
                            if i == SD - 1:
                                ins.then_inc(z_pe, 1)
                    eng.wait_ge(z_v, E_BM2B)
                    eng.wait_ge(z_ev, NGC + SD)
                    for jsub in range(SD):
                        for i in range(SD):
                            ins = eng.matmul(psd[:, jsub:jsub + 1],
                                             w_o1[:, D * i + 128 * jsub:D * i + 128 * jsub + 128],
                                             bm2b[:, i:i + 1],
                                             start=(i == 0), stop=(i == SD - 1))
                            if i == SD - 1 and jsub == SD - 1:
                                ins.then_inc(z_pe, 1)

                barrier(block0,
                        sync_waits=[(z_wcm, 48), (z_ev, NGC + SD + 1)])

        def phase1a():
            C1 = ExitStack()
            with C1:
                _lsems = []
                def sem(n, _st=C1, _ls=_lsems):
                    h = _st.enter_context(nc.semaphore(n))
                    _ls.append(h)
                    return h
                sb1 = lambda n, sh, dt: C1.enter_context(nc.sbuf_tensor(n, list(sh), dt))
                w_k = sb1("w_k1", (128, SD * D), BF16)
                w_v = sb1("w_v1", (128, SD * D), BF16)
                xbuf = [sb1("xbuf%d" % i, (128, 4 * D), F32) for i in range(2)]
                xc32 = sb1("xc32", (128, 4 * D), F32)
                sq32 = sb1("sq32", (128, D), F32)
                st4 = [sb1("st4_%d" % i, (128, 16), F32) for i in range(2)]
                xnb = [sb1("xnb%d" % i, (128, 4 * D), BF16) for i in range(2)]
                xf = [sb1("xf%d" % i, (128, SD * 512), BF16) for i in range(2)]
                ksb = [sb1("ksb%d" % i, (128, SD * 512), BF16) for i in range(2)]
                vsb = [sb1("vsb%d" % i, (128, SD * 512), BF16) for i in range(2)]
                vtb = [sb1("vtb%d" % i, (128, 4 * D), BF16) for i in range(2)]
                xob = [sb1("xob%d" % i, (128, D), F32) for i in range(2)]
                xoc = sb1("xoc", (128, D), F32)
                ost = [sb1("ost%d" % i, (128, 8), F32) for i in range(2)]
                xnob = [sb1("xnob%d" % i, (128, D), BF16) for i in range(2)]
                yq = sb1("yq", (128, D), F32)
                q8t = sb1("q8t", (128, D), FP8)
                abuf = sb1("abuf", (128, D), BF16)
                aout = [sb1("aout%d" % i, (128, D), BF16) for i in range(2)]
                ps1 = [C1.enter_context(nc.psum_tensor("p1ps%d" % i, [128, 512], F32)) for i in range(6)]
                HB = SD // 2

                a_wl = sem("a_wl")
                a_xld = [sem("a_xld0"), sem("a_xld1")]
                a_old = [sem("a_old0"), sem("a_old1")]
                a_xnw = [[sem("a_xnw%d%d" % (i, hh)) for hh in range(2)] for i in range(2)]
                a_txf = [sem("a_txf0"), sem("a_txf1")]
                a_txv = [sem("a_txv0"), sem("a_txv1")]
                a_var = sem("a_var")
                a_sq = sem("a_sq")
                a_xnr = sem("a_xnr")
                a_pe = sem("a_pe")
                a_ev = sem("a_ev")
                a_kw = [sem("a_kw0"), sem("a_kw1")]
                a_vfw = [sem("a_vfw0"), sem("a_vfw1")]
                a_vtw = [sem("a_vtw0"), sem("a_vtw1")]
                a_onw = [sem("a_onw0"), sem("a_onw1")]
                a_aw = [sem("a_aw0"), sem("a_aw1")]
                a_vch = sem("a_vch")
                a_och = sem("a_och")

                block1 = C1.enter_context(nc.Block())

                @block1.sync
                def _(eng):
                    eng.dma_start(out=w_k[:], in_=wpre[IWK, :, :]).then_inc(a_wl, 16)
                    eng.dma_start(out=w_v[:], in_=wpre[IWV, :, :]).then_inc(a_wl, 16)

                    def load_x(q):
                        eng.dma_start(out=xbuf[q % 2][:].rearrange("p (t d) -> p t d", d=D),
                                      in_=xb[512 * q:512 * q + 512, :].rearrange("(t p) d -> p t d", p=128)) \
                           .then_inc(a_xld[q % 2], 16)

                    def load_xo(q):
                        eng.dma_start(out=xob[q % 2][:], in_=xo[128 * q:128 * q + 128, :]) \
                           .then_inc(a_old[q % 2], 16)

                    def kv_writes(q):
                        sq = q % 2
                        eng.wait_ge(a_ev, 2 * SD * q + SD)
                        eng.dma_start(out=k_scr[q, :, :], in_=ksb[sq][:]).then_inc(a_kw[sq], 16)
                        eng.wait_ge(a_ev, 2 * SD * (q + 1))
                        for o in range(SD):
                            eng.dma_start(out=vfm_scr[q, 128 * o:128 * o + 128, :],
                                          in_=vsb[sq][:, 512 * o:512 * o + 512]).then_inc(a_vfw[sq], 16)

                    def vtb_chain(q):
                        sq = q % 2
                        if q >= 2:
                            eng.wait_ge(a_vtw[sq], 16 * (q // 2))
                        eng.wait_ge(a_vfw[sq], 16 * SD * (q // 2 + 1))
                        for t in range(4):
                            eng.dma_start_transpose(out=vtb[sq][:, D * t:D * t + D],
                                                    in_=vfm_scr[q, :, 128 * t:128 * t + 128]) \
                               .then_inc(a_txv[sq], 16)
                        eng.wait_ge(a_txv[sq], 16 * 4 * (q // 2 + 1))
                        eng.dma_start(out=v_scr[512 * q:512 * q + 512, :].rearrange("(t p) d -> p t d", p=128),
                                      in_=vtb[sq][:].rearrange("p (t d) -> p t d", d=D)) \
                           .then_inc(a_vtw[sq], 16)

                    for q in range(min(2, NP)):
                        load_x(q)
                        load_xo(q)
                    for p in range(NP):
                        s = p % 2
                        if p + 2 < NP:
                            eng.wait_ge(a_vch, p + 1)
                            load_x(p + 2)
                            eng.wait_ge(a_och, p + 1)
                            load_xo(p + 2)
                        eng.wait_ge(a_xnr, 3 * p + 1)
                        HD = D // 2
                        for hh in range(2):
                            eng.dma_start(out=xn_scr[512 * p:512 * p + 512, HD * hh:HD * hh + HD]
                                          .rearrange("(t p) d -> p t d", p=128),
                                          in_=xnb[s][:].rearrange("p (t d) -> p t d", d=D)[:, :, HD * hh:HD * hh + HD]) \
                               .then_inc(a_xnw[s][hh], 16)
                        eng.wait_ge(a_xnr, 3 * p + 2)
                        eng.dma_start(out=xno_scr[128 * p:128 * p + 128, :], in_=xnob[s][:]).then_inc(a_onw[s], 16)
                        eng.wait_ge(a_xnr, 3 * p + 3)
                        eng.dma_start(out=a_scr[128 * p:128 * p + 128, :], in_=aout[s][:]).then_inc(a_aw[s], 16)
                        if p >= 1:
                            kv_writes(p - 1)
                        if p >= 2:
                            vtb_chain(p - 2)
                    kv_writes(NP - 1)
                    for q in (NP - 2, NP - 1):
                        if q >= 0:
                            vtb_chain(q)

                @block1.vector
                def _(eng):
                    for p in range(NP):
                        s = p % 2
                        eng.wait_ge(a_xld[s], 16 * (p // 2 + 1))
                        x4 = xbuf[s][:].rearrange("p (t d) -> p t d", d=D)
                        ch_v(eng.reduce_sum(st4[s][:, 0:4].rearrange("p (t f) -> p t f", f=1), x4, axis=AX))
                        ch_v(eng.tensor_scalar(st4[s][:, 0:4], st4[s][:, 0:4], -1.0 / D, None, op0=OP.mult))
                        for t in range(4):
                            ch_v(eng.tensor_scalar(xc32[:, D * t:D * t + D], xbuf[s][:, D * t:D * t + D],
                                                   st4[s][:, t:t + 1], None, op0=OP.add))
                        ch_v(eng.tensor_tensor(xbuf[s][:], xc32[:], xc32[:], op=OP.mult))
                        ch_v(eng.reduce_sum(st4[s][:, 4:8].rearrange("p (t f) -> p t f", f=1),
                                            xbuf[s][:].rearrange("p (t d) -> p t d", d=D), axis=AX))
                        ch_v.signal(eng, a_vch, 1)
                        ch_v(eng.tensor_scalar(st4[s][:, 4:8], st4[s][:, 4:8], 1.0 / D, None, op0=OP.mult))
                        ch_v.signal(eng, a_var, 1)
                        eng.wait_ge(a_sq, 2 * p + 1)
                        if p >= 2:
                            eng.wait_ge(a_xnw[s][0], 16 * (p // 2))
                            eng.wait_ge(a_xnw[s][1], 16 * (p // 2))
                        ch_v(eng.reciprocal(st4[s][:, 8:12], st4[s][:, 12:16]))
                        for t in range(4):
                            ch_v(eng.tensor_scalar(xnb[s][:, D * t:D * t + D], xc32[:, D * t:D * t + D],
                                                   st4[s][:, 8 + t:9 + t], None, op0=OP.mult))
                        ch_v.signal(eng, a_xnr, 1)
                        eng.wait_ge(a_old[s], 16 * (p // 2 + 1))
                        ch_v(eng.reduce_sum(ost[s][:, 0:1], xob[s][:], axis=AX))
                        ch_v(eng.tensor_scalar(ost[s][:, 0:1], ost[s][:, 0:1], -1.0 / D, None, op0=OP.mult))
                        ch_v(eng.tensor_scalar(xoc[:], xob[s][:], ost[s][:, 0:1], None, op0=OP.add))
                        ch_v.signal(eng, a_och, 1)
                        ch_v(eng.tensor_tensor(sq32[:], xoc[:], xoc[:], op=OP.mult))
                        ch_v(eng.reduce_sum(ost[s][:, 1:2], sq32[:], axis=AX))
                        ch_v(eng.tensor_scalar(ost[s][:, 1:2], ost[s][:, 1:2], 1.0 / D, None, op0=OP.mult))
                        ch_v.signal(eng, a_var, 1)
                        eng.wait_ge(a_sq, 2 * p + 2)
                        if p >= 2:
                            eng.wait_ge(a_onw[s], 16 * (p // 2))
                            eng.wait_ge(a_aw[s], 16 * (p // 2))
                        ch_v(eng.reciprocal(ost[s][:, 2:3], ost[s][:, 3:4]))
                        ch_v(eng.tensor_scalar(xoc[:], xoc[:], ost[s][:, 2:3], None, op0=OP.mult))
                        ch_v(eng.tensor_copy(xnob[s][:], xoc[:]))
                        ch_v.signal(eng, a_xnr, 1)
                        ch_v(eng.tensor_reduce(ost[s][:, 4:5], xoc[:], axis=AX, op=OP.max,
                                               apply_absolute_value=True))
                        ch_v(eng.tensor_scalar(ost[s][:, 4:5], ost[s][:, 4:5], kc(K12), None, op0=OP.max))
                        ch_v(eng.reciprocal(ost[s][:, 5:6], ost[s][:, 4:5]))
                        ch_v(eng.tensor_scalar(ost[s][:, 5:6], ost[s][:, 5:6], 448.0, None, op0=OP.mult))
                        ch_v(eng.tensor_scalar(yq[:], xoc[:], ost[s][:, 5:6], 0.5, op0=OP.mult, op1=OP.mult))
                        ch_v(eng.tensor_copy(q8t[:], yq[:]))
                        ch_v(eng.reciprocal(ost[s][:, 6:7], ost[s][:, 5:6]))
                        ch_v(eng.tensor_scalar(ost[s][:, 6:7], ost[s][:, 6:7], 2.0, fws_sb[:, 0:1],
                                               op0=OP.mult, op1=OP.mult))
                        ch_v(eng.tensor_scalar(abuf[:], q8t[:], ost[s][:, 6:7], None, op0=OP.mult))
                        ch_v(eng.tensor_copy(aout[s][:].bitcast(U8), abuf[:].bitcast(U8)))
                        ch_v.signal(eng, a_xnr, 1)

                @block1.scalar
                def _(eng):
                    def sqrt_main(q):
                        eng.wait_ge(a_var, 2 * q + 1)
                        ch_s(eng.activation(st4[q % 2][:, 12:16], st4[q % 2][:, 4:8], AF.Sqrt, bias=kc(KEPS)))
                        ch_s.signal(eng, a_sq, 1)

                    def sqrt_own(q):
                        eng.wait_ge(a_var, 2 * q + 2)
                        ch_s(eng.activation(ost[q % 2][:, 3:4], ost[q % 2][:, 1:2], AF.Sqrt, bias=kc(KEPS)))
                        ch_s.signal(eng, a_sq, 1)

                    def transposes(q):
                        sq = q % 2
                        if q >= 2:
                            eng.wait_ge(a_pe, 2 * SD * (q - 1))
                        eng.wait_ge(a_xnw[sq][0], 16 * (q // 2 + 1))
                        for i in range(SD // 2):
                            eng.dma_start_transpose(out=xf[sq][:, 512 * i:512 * i + 512],
                                                    in_=xn_scr[512 * q:512 * q + 512, 128 * i:128 * i + 128]) \
                               .then_inc(a_txf[sq], 16)
                        eng.wait_ge(a_xnw[sq][1], 16 * (q // 2 + 1))
                        for i in range(SD // 2, SD):
                            eng.dma_start_transpose(out=xf[sq][:, 512 * i:512 * i + 512],
                                                    in_=xn_scr[512 * q:512 * q + 512, 128 * i:128 * i + 128]) \
                               .then_inc(a_txf[sq], 16)

                    POS_OWN = min(3, SD - 1)
                    POS_TR = min(5, SD - 1)
                    sqrt_main(0)
                    sqrt_own(0)
                    transposes(0)
                    for p in range(NP):
                        s = p % 2
                        if p >= 2:
                            eng.wait_ge(a_kw[s], 16 * (p // 2))
                        for o in range(SD):
                            G = 2 * SD * p + o
                            eng.wait_ge(a_pe, G + 1)
                            ch_s(eng.activation(ksb[s][:, 512 * o:512 * o + 512],
                                                ps1[G % 6][:, :],
                                                AF.Identity, bias=bcv(BK, o)))
                            ch_s.signal(eng, a_ev, 1)
                        if p + 1 < NP:
                            sqrt_main(p + 1)
                        if p >= 2:
                            eng.wait_ge(a_vfw[s], 16 * SD * (p // 2))
                        for o in range(SD):
                            G = 2 * SD * p + SD + o
                            eng.wait_ge(a_pe, G + 1)
                            ch_s(eng.activation(vsb[s][:, 512 * o:512 * o + 512],
                                                ps1[G % 6][:, :],
                                                AF.Identity, bias=bcv(BV, o)))
                            ch_s.signal(eng, a_ev, 1)
                            if o == POS_OWN and p + 1 < NP:
                                sqrt_own(p + 1)
                            if o == POS_TR and p + 1 < NP:
                                transposes(p + 1)

                @block1.tensor
                def _(eng):
                    eng.wait_ge(a_wl, 32)
                    for p in range(NP):
                        s = p % 2
                        eng.wait_ge(a_txf[s], 16 * SD * (p // 2 + 1))
                        for mat in range(2):
                            w = w_k if mat == 0 else w_v
                            for o in range(SD):
                                G = 2 * SD * p + SD * mat + o
                                if G >= 6:
                                    eng.wait_ge(a_ev, G - 5)
                                dst = ps1[G % 6][:, :]
                                for i in range(SD):
                                    ins = eng.matmul(dst,
                                                     w[:, D * i + 128 * o:D * i + 128 * o + 128],
                                                     xf[s][:, 512 * i:512 * i + 512],
                                                     start=(i == 0), stop=(i == SD - 1))
                                    if i == SD - 1:
                                        ins.then_inc(a_pe, 1)

                barrier(block1, sync_waits=[
                    (a_kw[0], 16 * (NP // 2)), (a_kw[1], 16 * (NP // 2)),
                    (a_vtw[0], 16 * (NP // 2)), (a_vtw[1], 16 * (NP // 2)),
                    (a_onw[0], 16 * (NP // 2)), (a_onw[1], 16 * (NP // 2)),
                    (a_aw[0], 16 * (NP // 2)), (a_aw[1], 16 * (NP // 2)),
                    (a_vfw[0], 16 * SD * (NP // 2)), (a_vfw[1], 16 * SD * (NP // 2))])

        def phase1b():
            C2 = ExitStack()
            with C2:
                _lsems = []
                def sem(n, _st=C2, _ls=_lsems):
                    h = _st.enter_context(nc.semaphore(n))
                    _ls.append(h)
                    return h
                sb2 = lambda n, sh, dt: C2.enter_context(nc.sbuf_tensor(n, list(sh), dt))
                w_q = sb2("w_q2", (128, SD * D), BF16)
                w_a1 = sb2("w_a12", (128, SD * D), BF16)
                w_b1 = sb2("w_b12", (128, SD * D), BF16)
                w_c = sb2("w_c2", (128, SD * D), BF16)
                w_bm = sb2("w_bm2", (128, SD * D), BF16)
                xof_ = sb2("xof", (128, SD * 512), BF16)
                af_ = sb2("af", (128, SD * 512), BF16)
                qsb_ = sb2("qsb", (128, SD * 512), BF16)
                ub = sb2("ub", (128, SD * 512), BF16)
                wcat = sb2("wcat", (128, SD * 528), BF16)
                gsum = sb2("gsum", (128, SD * 512), F32)
                zt = [sb2("zt%d" % i, (128, SD * 512), BF16) for i in range(2)]
                su = [sb2("su%d" % i, (128, SD * 512), BF16) for i in range(2)]
                gb = sb2("gb", (128, SD * 512), BF16)
                prt_ = sb2("prt", (128, SD * 512), BF16)
                ps2 = [C2.enter_context(nc.psum_tensor("p2ps%d" % i, [128, 512], F32)) for i in range(6)]

                b_wl = sem("b_wl")
                b_txx = sem("b_txx")
                b_ta = sem("b_ta")
                b_pe = sem("b_pe")
                b_ev = sem("b_ev")
                b_z = sem("b_z")
                b_sil = sem("b_sil")
                b_gad = sem("b_gad")
                b_gb = sem("b_gb")
                b_qs = sem("b_qs")
                b_qw = sem("b_qw")
                b_pw = sem("b_pw")

                GPE = 4 * SD
                HF = SD // 2
                HW = HF * 512
                zix = lambda g, w, h: 8 * g + (w - 1) + 4 * h + 1
                aix = lambda g, w, h: 6 * g + (w - 2) + 3 * h + 1
                block2 = C2.enter_context(nc.Block())

                @block2.sync
                def _(eng):
                    for (w, wi) in [(w_q, IWQ), (w_a1, IA1), (w_b1, IB1), (w_c, IWC), (w_bm, IBM)]:
                        eng.dma_start(out=w[:], in_=wpre[wi, :, :]).then_inc(b_wl, 16)
                    for g in range(NG):
                        eng.wait_ge(b_qs, 2 * g + 1)
                        eng.dma_start(out=q_scr[g, :, :], in_=qsb_[:]).then_inc(b_qw, 16)
                        eng.wait_ge(b_qs, 2 * g + 2)
                        eng.dma_start(out=part_scr[g, :, :], in_=prt_[:]).then_inc(b_pw, 16)

                @block2.scalar
                def _(eng):
                    def silu(g, w, h):
                        eng.wait_ge(b_z, zix(g, w, h))
                        if w == 1:
                            if g >= 1:
                                eng.wait_ge(b_gb, 2 * g)
                            ch_s(eng.activation(gsum[:, HW * h:HW * h + HW],
                                                zt[1][:, HW * h:HW * h + HW], AF.Silu))
                        else:
                            if w == 4:
                                eng.wait_ge(b_gad, aix(g, 2, h))
                            elif g >= 1:
                                eng.wait_ge(b_gad, aix(g - 1, 6 - w, h))
                            ch_s(eng.activation(su[w % 2][:, HW * h:HW * h + HW],
                                                zt[w % 2][:, HW * h:HW * h + HW], AF.Silu))
                        ch_s.signal(eng, b_sil, 1)

                    for g in range(NG):
                        if g >= 1:
                            eng.wait_ge(b_pe, GPE * (g - 1) + 3 * SD)
                        for i in range(SD):
                            eng.dma_start_transpose(out=xof_[:, 512 * i:512 * i + 512],
                                                    in_=xno_scr[512 * g:512 * g + 512, 128 * i:128 * i + 128]) \
                               .then_inc(b_txx, 16)
                        if g >= 1:
                            eng.wait_ge(b_pe, GPE * g)
                        for i in range(SD):
                            eng.dma_start_transpose(out=af_[:, 512 * i:512 * i + 512],
                                                    in_=a_scr[512 * g:512 * g + 512, 128 * i:128 * i + 128]) \
                               .then_inc(b_ta, 16)
                        for o in range(SD):
                            gg = GPE * g + o
                            eng.wait_ge(b_pe, gg + 1)
                            ch_s(eng.activation(ub[:, 512 * o:512 * o + 512], ps2[gg % 6][:, :],
                                                AF.Identity, bias=bcv(BM1, o)))
                            ch_s.signal(eng, b_ev, 1)
                        for o in range(SD):
                            gg = GPE * g + SD + o
                            eng.wait_ge(b_pe, gg + 1)
                            ch_s(eng.activation(wcat[:, 528 * o:528 * o + 528]
                                                .rearrange("p (j n) -> p j n", n=132)[:, :, 4:132],
                                                ps2[gg % 6][:, :].rearrange("p (j n) -> p j n", n=128),
                                                AF.Copy))
                            ch_s.signal(eng, b_ev, 1)
                        if g >= 1:
                            eng.wait_ge(b_qw, 16 * g)

                        def qevac(o):
                            gg = GPE * g + 2 * SD + o
                            eng.wait_ge(b_pe, gg + 1)
                            ch_s(eng.activation(qsb_[:, 512 * o:512 * o + 512], ps2[gg % 6][:, :],
                                                AF.Identity, bias=bcv(BQ, o), scale=kc(K32)))
                            ch_s.signal(eng, b_ev, 1)

                        qi = 0
                        for si, (w, h) in enumerate([(1, 0), (2, 0), (3, 0), (4, 0),
                                                     (1, 1), (2, 1), (3, 1), (4, 1)]):
                            silu(g, w, h)
                            if si >= 1 and qi < SD:
                                qevac(qi)
                                qi += 1
                        while qi < SD:
                            qevac(qi)
                            qi += 1
                        ch_s.signal(eng, b_qs, 1)
                        if g >= 1:
                            eng.wait_ge(b_pw, 16 * g)
                        for o in range(SD):
                            gg = GPE * g + 3 * SD + o
                            eng.wait_ge(b_pe, gg + 1)
                            ch_s(eng.activation(prt_[:, 512 * o:512 * o + 512], ps2[gg % 6][:, :],
                                                AF.Identity, bias=bcv(BOB, o)))
                            ch_s.signal(eng, b_ev, 1)
                        ch_s.signal(eng, b_qs, 1)

                @block2.vector
                def _(eng):
                    r4 = lambda t: t[:].rearrange("p (o j n) -> p o j n", j=4, n=128)
                    rw = lambda: wcat[:].rearrange("p (o j n) -> p o j n", j=4, n=132)

                    for g in range(NG):
                        ch_v(eng.tensor_copy(
                            rw()[:, :, :, 0:4],
                            whalo[:].rearrange("p (o t k) -> p o t k", t=32, k=4)[:, :, 4 * g:4 * g + 4, :]))

                        def zop(w, h):
                            if w == 1 and g >= 1:
                                eng.wait_ge(b_sil, zix(g - 1, 3, h))
                            if w == 2 and g >= 1:
                                eng.wait_ge(b_sil, zix(g - 1, 4, h))
                            if w == 3:
                                eng.wait_ge(b_sil, zix(g, 1, h))
                            if w == 4:
                                eng.wait_ge(b_sil, zix(g, 2, h))
                            if w == 1 and h == 0:
                                eng.wait_ge(b_ev, GPE * g + SD + HF)
                            if w == 1 and h == 1:
                                eng.wait_ge(b_ev, GPE * g + 2 * SD)
                            ch_v(eng.tensor_tensor(
                                r4(zt[w % 2])[:, HF * h:HF * h + HF],
                                r4(ub)[:, HF * h:HF * h + HF],
                                rw()[:, HF * h:HF * h + HF, :, 4 - w:132 - w],
                                op=OP.add))
                            ch_v.signal(eng, b_z, 1)

                        def gadd(w, h):
                            eng.wait_ge(b_sil, zix(g, w, h))
                            ch_v(eng.tensor_tensor(gsum[:, HW * h:HW * h + HW],
                                                   gsum[:, HW * h:HW * h + HW],
                                                   su[w % 2][:, HW * h:HW * h + HW], op=OP.add))
                            ch_v.signal(eng, b_gad, 1)

                        def gbcopy(h):
                            ch_v(eng.tensor_copy(gb[:, HW * h:HW * h + HW],
                                                 gsum[:, HW * h:HW * h + HW]))
                            ch_v.signal(eng, b_gb, 1)

                        zop(1, 0); zop(2, 0); zop(3, 0); zop(4, 0)
                        zop(1, 1); zop(2, 1)
                        if g >= 1:
                            eng.wait_ge(b_pe, GPE * (g - 1) + 4 * SD)
                        gadd(2, 0); gadd(3, 0); gadd(4, 0); gbcopy(0)
                        zop(3, 1); zop(4, 1)
                        gadd(2, 1); gadd(3, 1); gadd(4, 1); gbcopy(1)

                @block2.tensor
                def _(eng):
                    eng.wait_ge(b_wl, 16 * 5)
                    for g in range(NG):
                        eng.wait_ge(b_txx, 16 * SD * (g + 1))
                        for mat, w in ((0, w_a1), (1, w_b1), (2, w_q)):
                            for o in range(SD):
                                gg = GPE * g + SD * mat + o
                                if gg >= 6:
                                    eng.wait_ge(b_ev, gg - 5)
                                for i in range(SD):
                                    ins = eng.matmul(ps2[gg % 6][:, :],
                                                     w[:, D * i + 128 * o:D * i + 128 * o + 128],
                                                     xof_[:, 512 * i:512 * i + 512],
                                                     start=(i == 0), stop=(i == SD - 1))
                                    if i == SD - 1:
                                        ins.then_inc(b_pe, 1)
                        eng.wait_ge(b_gb, 2 * (g + 1))
                        eng.wait_ge(b_ta, 16 * SD * (g + 1))
                        for o in range(SD):
                            gg = GPE * g + 3 * SD + o
                            if gg >= 6:
                                eng.wait_ge(b_ev, gg - 5)
                            for i in range(SD):
                                eng.matmul(ps2[gg % 6][:, :],
                                           w_c[:, D * i + 128 * o:D * i + 128 * o + 128],
                                           gb[:, 512 * i:512 * i + 512],
                                           start=(i == 0), stop=False)
                            for i in range(SD):
                                ins = eng.matmul(ps2[gg % 6][:, :],
                                                 w_bm[:, D * i + 128 * o:D * i + 128 * o + 128],
                                                 af_[:, 512 * i:512 * i + 512],
                                                 start=False, stop=(i == SD - 1))
                                if i == SD - 1:
                                    ins.then_inc(b_pe, 1)

                barrier(block2, sync_waits=[(b_qw, 16 * NG), (b_pw, 16 * NG)])

        def phase2():
            C3 = ExitStack()
            with C3:
                _lsems = []
                def sem(n, _st=C3, _ls=_lsems):
                    h = _st.enter_context(nc.semaphore(n))
                    _ls.append(h)
                    return h
                sb3 = lambda n, sh, dt: C3.enter_context(nc.sbuf_tensor(n, list(sh), dt))
                qb = sb3("qb", (128, NG * SD * 512), BF16)
                kp = [sb3("kp%d" % i, (128, SD * 512), BF16) for i in range(3)]
                vb = sb3("vb", (128, 4 * TO * (TO + 1)), F32)
                ib = sb3("ib", (128, 4 * TO * (TO + 1)), F32)
                off = [4 * t * (t + 1) for t in range(TO + 1)]
                u32t = sb3("u32t", (128, 8), U32)
                f8 = sb3("f8", (128, 8), F32)
                e8 = sb3("e8", (128, 8), F32)
                p8 = [sb3("p8_%d" % i, (128, 8), F32) for i in range(2)]
                negm = sb3("negm", (128, 1), F32)
                lsum = sb3("lsum", (128, 1), F32)
                tmp8 = sb3("tmp8", (128, 8 * TO), F32)
                idm = sb3("idm", (128, 8), F32)
                idxi = sb3("idxi", (128, 8), I32)
                vsel = [sb3("vsel%d" % i, (128, 8 * D), BF16) for i in range(2)]
                gacc = [sb3("gacc%d" % i, (128, D), F32) for i in range(2)]
                gbf = [sb3("gbf%d" % i, (128, D), BF16) for i in range(2)]
                wtm = [sb3("wtm%d" % j, (128, D), BF16) for j in range(8)]
                ps3 = [C3.enter_context(nc.psum_tensor("p3ps%d" % i, [128, 512], F32)) for i in range(4)]

                c_ql = sem("c_ql")
                c_kp = [sem("c_kp0"), sem("c_kp1"), sem("c_kp2")]
                c_kf = sem("c_kf")
                c_un = sem("c_un")
                c_uv = sem("c_uv")
                c_f8 = sem("c_f8")
                c_ex = sem("c_ex")
                c_gt = sem("c_gt")
                c_gg = sem("c_gg")
                c_gw = [sem("c_gw0"), sem("c_gw1")]
                c_wm = sem("c_wm")
                c_wa = sem("c_wa")

                units = []
                for p in range(NP):
                    for j in range(p, TO):
                        units.append((p, j))
                uid = {pj: i for i, pj in enumerate(units)}
                NU = len(units)

                block3 = C3.enter_context(nc.Block())

                @block3.sync
                def _(eng):
                    for g in range(NG):
                        eng.dma_start(out=qb[:, SD * 512 * g:SD * 512 * (g + 1)],
                                      in_=q_scr[g, :, :]).then_inc(c_ql, 16)
                    def glob_write(t):
                        eng.wait_ge(c_gg, t + 1)
                        eng.dma_start(out=glob_scr[128 * t:128 * t + 128, :], in_=gbf[t % 2][:]) \
                           .then_inc(c_gw[t % 2], 16)

                    for p in range(NP):
                        if p >= 3:
                            eng.wait_ge(c_kf, p - 2)
                        eng.dma_start(out=kp[p % 3][:], in_=k_scr[p, :, :]).then_inc(c_kp[p % 3], 16)
                        if p >= 3:
                            glob_write(p - 3)
                    for t in range(max(0, TO - 3), TO):
                        glob_write(t)

                @block3.tensor
                def _(eng):
                    eng.wait_ge(c_ql, 16 * NG)
                    for p in range(NP):
                        eng.wait_ge(c_kp[p % 3], 16 * (p // 3 + 1))
                        for j in range(p, TO):
                            u = uid[(p, j)]
                            if u >= 4:
                                eng.wait_ge(c_uv, u - 3)
                            g4, j4 = divmod(j, 4)
                            for i in range(SD):
                                ins = eng.matmul(ps3[u % 4][:, :],
                                                 qb[:, SD * 512 * g4 + 512 * i + 128 * j4:
                                                     SD * 512 * g4 + 512 * i + 128 * j4 + 128],
                                                 kp[p % 3][:, 512 * i:512 * i + 512],
                                                 start=(i == 0), stop=(i == SD - 1))
                                if i == SD - 1:
                                    ins.then_inc(c_un, 1)
                        i = eng.sem_inc(c_kf, 1)
                        i._wait_ge(c_un, uid[(p, TO - 1)] + 1)

                @block3.vector
                def _(eng):
                    def wsum(tw):
                        if tw >= 2:
                            eng.wait_ge(c_gw[tw % 2], 16 * (tw // 2))
                        for j in range(8):
                            eng.wait_ge(c_wm, 8 * tw + j + 1)
                            if j == 0:
                                ch_v(eng.tensor_copy(gacc[tw % 2][:], wtm[0][:]))
                            elif j < 7:
                                ch_v(eng.tensor_tensor(gacc[tw % 2][:], gacc[tw % 2][:],
                                                       wtm[j][:], op=OP.add))
                            else:
                                ch_v(eng.tensor_tensor(gbf[tw % 2][:], gacc[tw % 2][:],
                                                       wtm[7][:], op=OP.add))
                            ch_v.signal(eng, c_wa, 1)
                        ch_v.signal(eng, c_gg, 1)

                    for p in range(NP):
                        for j in range(p, TO):
                            u = uid[(p, j)]
                            eng.wait_ge(c_un, u + 1)
                            if j == p:
                                ch_v(eng.tensor_tensor(ps3[u % 4][:, :], ps3[u % 4][:, :],
                                                       dmask[:], op=OP.add))
                            ch_v(eng.max(out=vb[:, off[j] + 8 * p:off[j] + 8 * p + 8],
                                         in_=ps3[u % 4][:, :]))
                            ch_v(eng.max_index(out=u32t[:], in_max=vb[:, off[j] + 8 * p:off[j] + 8 * p + 8],
                                               in_values=ps3[u % 4][:, :]))
                            ch_v(eng.tensor_scalar(ib[:, off[j] + 8 * p:off[j] + 8 * p + 8],
                                                   u32t[:], float(512 * p - BIGI), None, op0=OP.add))
                            ch_v.signal(eng, c_uv, 1)
                        if p >= 2:
                            wsum(p - 2)
                        t = p
                        if t < TO:
                            W8 = 8 * (t + 1)
                            eng.wait_ge(c_gt, 128 * t)
                            ch_v(eng.max(out=f8[:], in_=vb[:, off[t]:off[t] + W8]))
                            ch_v(eng.tensor_scalar(negm[:, 0:1], f8[:, 0:1], -1.0, None, op0=OP.mult))
                            ch_v.signal(eng, c_f8, 1)
                            eng.wait_ge(c_ex, t + 1)
                            ch_v(eng.reduce_sum(lsum[:, 0:1], e8[:], axis=AX))
                            ch_v(eng.reciprocal(lsum[:, 0:1], lsum[:, 0:1]))
                            ch_v(eng.tensor_scalar(p8[t % 2][:], e8[:], lsum[:, 0:1], None, op0=OP.mult))
                            for k in range(8):
                                ch_v(eng.scalar_tensor_tensor(tmp8[:, 0:W8], vb[:, off[t]:off[t] + W8],
                                                              f8[:, k:k + 1], ib[:, off[t]:off[t] + W8],
                                                              op0=OP.is_equal, op1=OP.mult))
                                ch_v(eng.tensor_reduce(idm[:, k:k + 1], tmp8[:, 0:W8], axis=AX, op=OP.min))
                            ch_v(eng.tensor_scalar(idm[:], idm[:], BIGI, None, op0=OP.add))
                            ch_v(eng.tensor_copy(idxi[:], idm[:]))
                            ch_v.signal(eng, c_f8, 1)
                    if TO >= 2:
                        wsum(TO - 2)
                    wsum(TO - 1)

                @block3.scalar
                def _(eng):
                    for t in range(TO):
                        eng.wait_ge(c_f8, 2 * t + 1)
                        if t >= 1:
                            eng.wait_ge(c_f8, 2 * t)
                        ch_s(eng.activation(e8[:], f8[:], AF.Exp, bias=negm[:, 0:1]))
                        ch_s.signal(eng, c_ex, 1)
                        eng.wait_ge(c_f8, 2 * t + 2)
                        eng.wait_ge(c_gt, 128 * (t + 1))
                        for j in range(8):
                            if t >= 1:
                                eng.wait_ge(c_wa, 8 * (t - 1) + j + 1)
                            ch_s(eng.activation(wtm[j][:], vsel[t % 2][:, D * j:D * j + D],
                                                AF.Identity, bias=kc(KZ), scale=p8[t % 2][:, j:j + 1]))
                            ch_s.signal(eng, c_wm, 1)

                @block3.gpsimd
                def _(eng):
                    for t in range(TO):
                        eng.wait_ge(c_f8, 2 * t + 2)
                        if t >= 2:
                            eng.wait_ge(c_wm, 8 * (t - 1))
                        for j in range(8):
                            ins = eng.indirect_dma_start(
                                out=vsel[t % 2][:, D * j:D * j + D], out_offset=None,
                                in_=v_scr[:, :], in_offset=IndirectOffsetOnAxis(idxi[:, j:j + 1], 0))
                            ins.then_inc(c_gt, 16)

                barrier(block3, sync_waits=[(c_gw[0], 16 * (TO // 2)), (c_gw[1], 16 * (TO // 2))])

        def phase3():
            C4 = ExitStack()
            with C4:
                _lsems = []
                def sem(n, _st=C4, _ls=_lsems):
                    h = _st.enter_context(nc.semaphore(n))
                    _ls.append(h)
                    return h
                sb4 = lambda n, sh, dt: C4.enter_context(nc.sbuf_tensor(n, list(sh), dt))
                w_o2 = sb4("w_o24", (128, SD * D), BF16)
                gf = [sb4("gf%d" % i, (128, SD * 512), BF16) for i in range(2)]
                pb = [sb4("pb%d" % i, (128, SD * 512), BF16) for i in range(2)]
                res = [sb4("res%d" % i, (128, SD * 512), BF16) for i in range(2)]
                rt = [sb4("rt%d" % i, (128, 4 * D), BF16) for i in range(2)]
                xo4 = [sb4("xo4%d" % i, (128, 4 * D), F32) for i in range(2)]
                ob = [sb4("ob%d" % i, (128, 4 * D), F32) for i in range(2)]
                ps4 = [C4.enter_context(nc.psum_tensor("p4ps%d" % i, [128, 512], F32)) for i in range(4)]

                d_wl = sem("d_wl")
                d_tx = [sem("d_tx0"), sem("d_tx1")]
                d_pl = [sem("d_pl0"), sem("d_pl1")]
                d_xl = [sem("d_xl0"), sem("d_xl1")]
                d_pe = sem("d_pe")
                d_evo = sem("d_evo")
                d_res = sem("d_res")
                d_ad = sem("d_ad")
                d_rw = [sem("d_rw0"), sem("d_rw1")]
                d_rt = [sem("d_rt0"), sem("d_rt1")]
                d_ow = [sem("d_ow0"), sem("d_ow1")]

                block4 = C4.enter_context(nc.Block())

                @block4.sync
                def _(eng):
                    eng.dma_start(out=w_o2[:], in_=wpre[IO2, :, :]).then_inc(d_wl, 16)
                    def out_write(q):
                        eng.wait_ge(d_ad, q + 1)
                        eng.dma_start(out=out_d[512 * q:512 * q + 512, :].rearrange("(t p) d -> p t d", p=128),
                                      in_=ob[q % 2][:].rearrange("p (t d) -> p t d", d=D)) \
                           .then_inc(d_ow[q % 2], 16)
                    for g in range(NG):
                        s = g % 2
                        if g >= 2:
                            eng.wait_ge(d_res, g - 1)
                        eng.dma_start(out=pb[s][:], in_=part_scr[g, :, :]).then_inc(d_pl[s], 16)
                        if g >= 2:
                            eng.wait_ge(d_ad, g - 1)
                        eng.dma_start(out=xo4[s][:].rearrange("p (t d) -> p t d", d=D),
                                      in_=xo[512 * g:512 * g + 512, :].rearrange("(t p) d -> p t d", p=128)) \
                           .then_inc(d_xl[s], 16)
                        eng.wait_ge(d_res, g + 1)
                        for o in range(SD):
                            eng.dma_start(out=res_scr[g, 128 * o:128 * o + 128, :],
                                          in_=res[s][:, 512 * o:512 * o + 512]).then_inc(d_rw[s], 16)
                        if g >= 1:
                            out_write(g - 1)
                    out_write(NG - 1)
                    eng.wait_ge(d_ow[0], 16 * ((NG + 1) // 2))
                    if NG >= 2:
                        eng.wait_ge(d_ow[1], 16 * (NG // 2))

                @block4.scalar
                def _(eng):
                    def rt_tx(q):
                        if q >= 2:
                            eng.wait_ge(d_ad, q - 1)
                        eng.wait_ge(d_rw[q % 2], 16 * SD * (q // 2 + 1))
                        for t in range(4):
                            eng.dma_start_transpose(out=rt[q % 2][:, D * t:D * t + D],
                                                    in_=res_scr[q, :, 128 * t:128 * t + 128]) \
                               .then_inc(d_rt[q % 2], 16)
                    for g in range(NG):
                        s = g % 2
                        if g >= 2:
                            eng.wait_ge(d_res, g - 1)
                        for i in range(SD):
                            eng.dma_start_transpose(out=gf[s][:, 512 * i:512 * i + 512],
                                                    in_=glob_scr[512 * g:512 * g + 512, 128 * i:128 * i + 128]) \
                               .then_inc(d_tx[s], 16)
                        if g >= 1:
                            rt_tx(g - 1)
                    rt_tx(NG - 1)

                @block4.vector
                def _(eng):
                    def xadd(q):
                        if q >= 2:
                            eng.wait_ge(d_ow[q % 2], 16 * (q // 2))
                        eng.wait_ge(d_rt[q % 2], 16 * 4 * (q // 2 + 1))
                        eng.wait_ge(d_xl[q % 2], 16 * (q // 2 + 1))
                        for t in range(4):
                            ch_v(eng.tensor_tensor(ob[q % 2][:, D * t:D * t + D], rt[q % 2][:, D * t:D * t + D],
                                                   xo4[q % 2][:, D * t:D * t + D], op=OP.add))
                        ch_v.signal(eng, d_ad, 1)
                    for g in range(NG):
                        s = g % 2
                        if g >= 2:
                            eng.wait_ge(d_rw[s], 16 * SD * (g // 2))
                        for o in range(SD):
                            gg = SD * g + o
                            eng.wait_ge(d_pe, gg + 1)
                            if o == 0:
                                eng.wait_ge(d_pl[s], 16 * (g // 2 + 1))
                            ch_v(eng.tensor_tensor(res[s][:, 512 * o:512 * o + 512], ps4[gg % 4][:, :],
                                                   pb[s][:, 512 * o:512 * o + 512], op=OP.add))
                            ch_v.signal(eng, d_evo, 1)
                        ch_v.signal(eng, d_res, 1)
                        if g >= 1:
                            xadd(g - 1)
                    xadd(NG - 1)

                @block4.tensor
                def _(eng):
                    eng.wait_ge(d_wl, 16)
                    for g in range(NG):
                        s = g % 2
                        eng.wait_ge(d_tx[s], 16 * SD * (g // 2 + 1))
                        for o in range(SD):
                            gg = SD * g + o
                            if gg >= 4:
                                eng.wait_ge(d_evo, gg - 3)
                            for i in range(SD):
                                ins = eng.matmul(ps4[gg % 4][:, :],
                                                 w_o2[:, D * i + 128 * o:D * i + 128 * o + 128],
                                                 gf[s][:, 512 * i:512 * i + 512],
                                                 start=(i == 0), stop=(i == SD - 1))
                                if i == SD - 1:
                                    ins.then_inc(d_pe, 1)

                barrier(block4, sync_waits=[(d_ow[0], 16 * ((NG + 1) // 2))]
                        + ([(d_ow[1], 16 * (NG // 2))] if NG >= 2 else []))

        phase0()
        phase1a()
        phase1b()
        phase2()
        phase3()
    return nc


_CACHE = {}


def _get_prog(cfg_key):
    if cfg_key not in _CACHE:
        _CACHE[cfg_key] = build_program(CFG(*cfg_key))
    return _CACHE[cfg_key]


def kernel(L0_attn_mu, L0_attn_Wq, L0_attn_bq, L0_attn_Wk, L0_attn_bk,
           L0_attn_Wv, L0_attn_bv, L0_attn_Wm1, L0_attn_bm1, L0_attn_Wm2,
           L0_attn_bm2, L0_attn_Wo, L0_attn_bo, L0_ffn_weight,
           L0_ffn_weight_scale, L0_ffn_bias):
    import ml_dtypes

    x = np.ascontiguousarray(np.asarray(L0_attn_mu, dtype=np.float32))
    B, N, D = x.shape
    c = CFG(D=D, N=N)
    nc = _get_prog((D, N))

    f32 = lambda a: np.ascontiguousarray(np.asarray(a, dtype=np.float32))
    w8 = np.asarray(L0_ffn_weight)
    w8b = np.ascontiguousarray(w8.view(np.uint8))
    ws = np.full((128, 1), np.float32(np.asarray(L0_ffn_weight_scale)), dtype=np.float32)

    shared = dict(
        wq=f32(L0_attn_Wq), wk=f32(L0_attn_Wk), wv=f32(L0_attn_Wv),
        bq=f32(L0_attn_bq), bk=f32(L0_attn_bk), bv=f32(L0_attn_bv),
        wm1=f32(L0_attn_Wm1), bm1=f32(L0_attn_bm1), wm2=f32(L0_attn_Wm2),
        bm2=f32(L0_attn_bm2), wo=f32(L0_attn_Wo), bo=f32(L0_attn_bo),
        fw8=w8b, fws=ws, fb=f32(L0_ffn_bias),
    )

    in_maps = []
    for core in range(8):
        b, r = core // 4, core % 4
        xb_c = x[b]
        tbs = [4 * j + r for j in range(c.TO)]
        xo_c = np.concatenate([xb_c[128 * tb:128 * tb + 128] for tb in tbs], 0)
        halo = np.zeros((128, D), dtype=np.float32)
        for j, tb in enumerate(tbs):
            if tb > 0:
                halo[4 * j:4 * j + 4] = xb_c[128 * tb - 4:128 * tb]
        dm = np.zeros((128, 512), dtype=np.float32)
        cols = np.arange(512)[None, :]
        rows = np.arange(128)[:, None]
        dm[cols > 128 * r + rows] = NEG
        m = dict(shared)
        m.update(xb=np.ascontiguousarray(xb_c), xo=np.ascontiguousarray(xo_c),
                 xh=halo, dmask=dm)
        in_maps.append(m)

    res = run_bass_kernel_spmd(nc, in_maps, core_ids=list(range(8)))
    out = np.empty((B, N, D), dtype=np.float32)
    for core in range(8):
        b, r = core // 4, core % 4
        oc = np.asarray(res.results[core]["out"])
        for j in range(c.TO):
            tb = 4 * j + r
            out[b, 128 * tb:128 * tb + 128] = oc[128 * j:128 * j + 128]
    return out
